# version 17
# baseline (speedup 1.0000x reference)
import numpy as np
import concourse.bass as bass
import concourse.mybir as mybir
from concourse.bass_utils import run_bass_kernel_spmd

F32 = mybir.dt.float32
BF16 = mybir.dt.bfloat16
AF = mybir.ActivationFunctionType
ALU = mybir.AluOpType

D = 1024
DC = 8
HID = 4096
HC = 32
T = 512
NT = 8
N_CORES = 8
EPS_LN = 1e-5
EPS_Z = 1e-6


class Sem:
    def __init__(self, handle, name):
        self.h, self.name, self.n = handle, name, 0
        self.n_replay = 0


class Rec:

    def __init__(self, name):
        self.name = name
        self.ops = []
        self._lastw = {}

    def wait(self, sem, val):
        if val <= 0:
            return
        if self._lastw.get(sem.name, -1) >= val:
            return
        self._lastw[sem.name] = val
        import sys as _s
        self.ops.append(("w", sem, val,
                         (_s._getframe(1).f_lineno, _s._getframe(2).f_lineno if _s._getframe(1).f_back else 0)))

    def waits(self, lst):
        for s, v in lst:
            self.wait(s, v)

    def do(self, fn, sem=None, amt=1):
        import sys as _s
        self.ops.append(("e", fn, sem, amt,
                         (_s._getframe(1).f_lineno, _s._getframe(2).f_lineno if _s._getframe(1).f_back else 0)))
        if sem is not None:
            sem.n += amt
            return sem.n
        return None

    def scope(self, name):
        self.ops.append(("s", name, None))

    def inc_last(self, sem, amt=1):
        op = self.ops[-1]
        assert op[0] == "e" and op[2] is None
        self.ops[-1] = ("e", op[1], sem, amt)
        sem.n += amt
        return sem.n


def _simulate(recs, tag=""):
    vals = {}
    pcs = [0] * len(recs)
    progress = True
    while progress:
        progress = False
        for i, r in enumerate(recs):
            while pcs[i] < len(r.ops):
                op = r.ops[pcs[i]]
                if op[0] == "w":
                    if vals.get(op[1].name, 0) < op[2]:
                        break
                else:
                    if op[2] is not None:
                        vals[op[2].name] = vals.get(op[2].name, 0) + op[3]
                pcs[i] += 1
                progress = True
    stuck = {}
    for i, r in enumerate(recs):
        if pcs[i] < len(r.ops):
            op = r.ops[pcs[i]]
            stuck[r.name] = (pcs[i], len(r.ops),
                             op[1].name if op[0] == "w" else None,
                             op[2] if op[0] == "w" else None,
                             op[3] if op[0] == "w" and len(op) > 3 else None)
    if stuck:
        for i, r in enumerate(recs):
            if pcs[i] < len(r.ops):
                print(f"--- {r.name} pc {pcs[i]}: context:")
                for k in range(max(0, pcs[i] - 3), min(len(r.ops), pcs[i] + 2)):
                    op = r.ops[k]
                    print("   ", k, op[0], op[1] if op[0] != "e" else "",
                          op[2] if op[0] == "w" else "", op[-1] if isinstance(op[-1], tuple) else "")
    assert not stuck, f"DEADLOCK {tag}: {stuck}; sems={vals}"


def build(nt=NT, paranoid=False):
    nc = bass.Bass("TRN2", target_bir_lowering=False, debug=False,
                   detect_race_conditions=paranoid)
    NTOK = nt * T

    ap = lambda h: h.ap()
    xt = ap(nc.dram_tensor("xt", [D, NTOK], F32, kind="ExternalInput"))
    wq = ap(nc.dram_tensor("wq", [D, D], F32, kind="ExternalInput"))
    wk = ap(nc.dram_tensor("wk", [D, D], F32, kind="ExternalInput"))
    wv = ap(nc.dram_tensor("wv", [D, D], F32, kind="ExternalInput"))
    wo = ap(nc.dram_tensor("wo", [D, D], F32, kind="ExternalInput"))
    bqr = ap(nc.dram_tensor("bqr", [1, D], F32, kind="ExternalInput"))
    bot = ap(nc.dram_tensor("bot", [128, DC], F32, kind="ExternalInput"))
    bk = ap(nc.dram_tensor("bk", [1, D], F32, kind="ExternalInput"))
    bv = ap(nc.dram_tensor("bv", [1, D], F32, kind="ExternalInput"))
    w1t = ap(nc.dram_tensor("w1t", [HC, 128, DC, 128], F32, kind="ExternalInput"))
    w2t = ap(nc.dram_tensor("w2t", [HC, 128, DC, 128], F32, kind="ExternalInput"))
    h1t = ap(nc.dram_tensor("h1t", [HC, 128, HC, 128], F32, kind="ExternalInput"))
    h2t = ap(nc.dram_tensor("h2t", [HC, 128, HC, 128], F32, kind="ExternalInput"))
    w3t = ap(nc.dram_tensor("w3t", [DC, 128, HC, 128], F32, kind="ExternalInput"))
    s1 = ap(nc.dram_tensor("s1", [128, HC], F32, kind="ExternalInput"))
    s2 = ap(nc.dram_tensor("s2", [128, HC], F32, kind="ExternalInput"))
    sh1 = ap(nc.dram_tensor("sh1", [128, HC], F32, kind="ExternalInput"))
    sh2 = ap(nc.dram_tensor("sh2", [128, HC], F32, kind="ExternalInput"))
    s3 = ap(nc.dram_tensor("s3", [128, DC], F32, kind="ExternalInput"))
    selc = ap(nc.dram_tensor("selc", [16, DC * 128], F32, kind="ExternalInput"))
    indc = ap(nc.dram_tensor("indc", [128, DC * 16], F32, kind="ExternalInput"))
    yt = ap(nc.dram_tensor("yt", [D, NTOK], F32, kind="ExternalOutput"))

    w1b = ap(nc.dram_tensor("w1b", [HC, 128, DC, 128], BF16, kind="Internal"))
    w2b = ap(nc.dram_tensor("w2b", [HC, 128, DC, 128], BF16, kind="Internal"))
    h1b = ap(nc.dram_tensor("h1b", [HC, 128, HC, 128], BF16, kind="Internal"))
    h2b = ap(nc.dram_tensor("h2b", [HC, 128, HC, 128], BF16, kind="Internal"))
    w3b = ap(nc.dram_tensor("w3b", [DC, 128, HC, 128], BF16, kind="Internal"))
    qscr = ap(nc.dram_tensor("qscr", [nt, DC, 128, T], BF16, kind="Internal"))

    xtr = xt.rearrange("(c p) n -> p c n", p=128)
    ytr = yt.rearrange("(c p) n -> p c n", p=128)

    A = lambda name, shape, dt: nc.alloc_sbuf_tensor(name, shape, dt)
    per_part_total = nc.sbuf_bytes_remaining
    pan0 = A("pan0", [128, 4096], BF16)
    pan1 = A("pan1", [128, 4096], BF16)
    pan2 = A("pan2", [128, 4096], BF16)
    su = A("su", [128, 2048], F32)
    r0 = A("r0", [1, T], F32)
    r1 = A("r1", [1, T], F32)
    r2 = A("r2", [1, T], F32)
    zrb = A("zrb", [16, T], BF16)
    zbs = A("zbs", [128, T], BF16)
    zbs2 = A("zbs2", [128, T], BF16)
    Rb = A("Rb", [128, T], BF16)
    MRb = A("MRb", [128, T], BF16)
    kvT = A("kvT", [128, DC * 128], BF16)
    kcol0 = A("kcol0", [128, DC], F32)
    kcol1 = A("kcol1", [128, DC], F32)
    sel = A("sel", [16, DC * 128], BF16)
    ind = A("ind", [128, DC * 16], BF16)
    zlT = A("zlT", [128, DC * 16], BF16)
    bos = A("bos", [128, DC], F32)
    s1s = A("s1s", [128, HC], F32)
    s2s = A("s2s", [128, HC], F32)
    sh1s = A("sh1s", [128, HC], F32)
    sh2s = A("sh2s", [128, HC], F32)
    s3s = A("s3s", [128, DC], F32)
    onesb = A("onesb", [128, 1], BF16)
    ones1f = A("ones1f", [1, 128], F32)
    ones1b = A("ones1b", [1, 128], BF16)
    wos = A("wos", [128, DC * D], BF16)

    auto_used = per_part_total - nc.sbuf_bytes_remaining
    manual_need = 122880
    mbase = (per_part_total - manual_need) & ~31
    assert auto_used + 16512 <= mbase, \
        f"SBUF overflow: auto {auto_used}+16512 > base {mbase}"
    M = lambda name, shape, dt, off: nc.alloc_sbuf_tensor_at(name, shape, dt, offset=mbase + off)
    wqs = M("wqs", [128, DC * D], BF16, 0)
    wks = M("wks", [128, DC * D], BF16, 16384)
    wvs = M("wvs", [128, DC * D], BF16, 32768)
    xslab = M("xslab", [128, DC * T], F32, 49152)
    xbf = M("xbf", [128, DC * T], BF16, 65536)
    xsq = M("xsq", [128, DC * T], BF16, 73728)
    ln1 = M("ln1", [128, DC * T], BF16, 81920)
    kslab = M("kslab", [128, 4 * D], BF16, 90112)
    vslab = M("vslab", [128, 4 * D], BF16, 98304)
    qslab = M("qslab", [128, DC * T], BF16, 106496)
    hsl = M("hsl", [128, HC * T], BF16, 0)
    outsl = M("outsl", [128, DC * T], F32, 49152)
    gsl = M("gsl", [128, HC * T], BF16, 32768)
    Xsl = M("Xsl", [128, DC * T], F32, 65536)
    Yb0 = M("Yb0", [128, DC * T], BF16, 81920)
    Yb1 = M("Yb1", [128, DC * T], BF16, 90112)
    qt = M("qt", [128, DC * T], BF16, 98304)
    shsl = M("shsl", [128, DC * T], BF16, 106496)
    aosl = M("aosl", [128, DC * T], BF16, 114688)
    bqb = M("bqb", [1, D], BF16, 114688)
    bkb = M("bkb", [1, D], BF16, 116736)
    bvb = M("bvb", [1, D], BF16, 118784)
    ones512b = M("ones512b", [1, T], BF16, 120832)

    PA = nc.alloc_psum_tensor("PA", [128, 1024], F32)
    PB = nc.alloc_psum_tensor("PB", [128, 1024], F32)
    PC = nc.alloc_psum_tensor("PC", [128, 1024], F32)
    PD = nc.alloc_psum_tensor("PD", [128, 1024], F32)
    PA0, PA1 = PA[:, 0:512], PA[:, 512:1024]
    PB0, PB1 = PB[:, 0:512], PB[:, 512:1024]
    PC0, PC1 = PC[:, 0:512], PC[:, 512:1024]
    PD0, PD1 = PD[:, 0:512], PD[:, 512:1024]

    from contextlib import ExitStack
    _sem_ctx = ExitStack()
    mks = lambda name: Sem(_sem_ctx.enter_context(nc.semaphore(name)), name)
    cPE, cACT, cDVE = mks("cPE"), mks("cACT"), mks("cDVE")
    dAux = mks("dAux")
    dWrest, dWq, dWk, dWv, dWo = mks("dWrest"), mks("dWq"), mks("dWk"), mks("dWv"), mks("dWo")
    dPw1, dPw2, dPh1, dPh2, dPw3 = mks("dPw1"), mks("dPw2"), mks("dPh1"), mks("dPh2"), mks("dPw3")
    dS = [mks(f"dS{i}") for i in range(6)]
    dX, dQ, dQS, dOut = mks("dX"), mks("dQ"), mks("dQS"), mks("dOut")
    all_sems = [cPE, cACT, cDVE, dAux, dWrest, dWq, dWk, dWv, dWo,
                dPw1, dPw2, dPh1, dPh2, dPw3] + dS + [dX, dQ, dQS, dOut]

    SY, GP, PE, SC, VE = Rec("sync"), Rec("gpsimd"), Rec("tensor"), Rec("scalar"), Rec("vector")

    def dma(rec, out_ap, in_ap, sem):
        return rec.do(lambda e, o=out_ap, i=in_ap: e.dma_start(out=o, in_=i), sem, 16)

    dma(SC, bos[:], bot, dAux)
    dma(SC, s1s[:], s1, dAux)
    dma(SC, s2s[:], s2, dAux)
    dma(SC, sh1s[:], sh1, dAux)
    dma(SC, sh2s[:], sh2, dAux)
    dma(SC, s3s[:], s3, dAux)
    v_aux = dAux.n
    SC.wait(dAux, v_aux)

    dma(GP, sel[:], selc, dWrest)
    dma(GP, ind[:], indc, dWrest)
    dma(GP, bqb[:], bqr, dWrest)
    dma(GP, bkb[:], bk, dWrest)
    dma(GP, bvb[:], bv, dWrest)
    v_wrest = dWrest.n
    xsl3 = xslab[:].rearrange("p (c n) -> p c n", c=DC)
    v_x0 = dma(SY, xsl3, xtr[:, :, 0:T], dX)

    wq_r = wq.rearrange("(kc kp) o -> kp kc o", kp=128)
    wk_r = wk.rearrange("(kc kp) o -> kp kc o", kp=128)
    wv_r = wv.rearrange("(kc kp) o -> kp kc o", kp=128)
    wo_r = wo.rearrange("(kc kp) o -> kp kc o", kp=128)
    v_wq = dma(GP, wqs[:].rearrange("p (kc o) -> p kc o", kc=DC), wq_r, dWq)
    v_wk = dma(GP, wks[:].rearrange("p (kc o) -> p kc o", kc=DC), wk_r, dWk)
    v_wv = dma(GP, wvs[:].rearrange("p (kc o) -> p kc o", kc=DC), wv_r, dWv)
    v_wo = dma(GP, wos[:].rearrange("p (kc o) -> p kc o", kc=DC), wo_r, dWo)
    v_pw1 = 16
    v_pw2 = 16
    v_ph1 = 112
    v_ph2 = 128
    v_pw3 = 32

    def prep_w1():
        dma(GP, w1b, w1t, dPw1)

    def prep_w2():
        dma(GP, w2b, w2t, dPw2)

    def prep_h1(j, half=None):
        if half is None:
            dma(GP, h1b[j * 8:(j + 1) * 8], h1t[j * 8:(j + 1) * 8], dPh1)
        else:
            o = j * 8 + half * 4
            dma(GP, h1b[o:o + 4], h1t[o:o + 4], dPh1)

    def prep_h2(j, half=None):
        if half is None:
            dma(GP, h2b[j * 8:(j + 1) * 8], h2t[j * 8:(j + 1) * 8], dPh2)
        else:
            o = j * 8 + half * 4
            dma(GP, h2b[o:o + 4], h2t[o:o + 4], dPh2)

    def prep_w3(j):
        dma(GP, w3b[j * 4:(j + 1) * 4], w3t[j * 4:(j + 1) * 4], dPw3)

    VE.wait(dAux, v_aux)
    VE.do(lambda e: e.memset(onesb[:], 1.0), cDVE)
    VE.do(lambda e: e.memset(ones1f[:], 1.0), cDVE)
    VE.do(lambda e: e.memset(ones1b[:], 1.0), cDVE)
    VE.do(lambda e: e.memset(ones512b[:], 1.0), cDVE)

    def rows(m_psum, sq_psum, g_wait):
        VE.wait(cPE, g_wait)
        VE.do(lambda e, o=r0[:], i=m_psum: e.tensor_scalar_mul(o, i, 1.0 / D), cDVE)
        VE.do(lambda e, o=r1[:], a=r0[:]: e.tensor_tensor(o, a, a, ALU.mult), cDVE)
        VE.do(lambda e, o=r2[:], i=r1[:]: e.tensor_single_scalar(o, i, EPS_LN, ALU.subtract), cDVE)
        e_var = VE.do(lambda e, o=r1[:], i=sq_psum, m=r2[:]:
                      e.scalar_tensor_tensor(o, i, 1.0 / D, m, ALU.mult, ALU.subtract), cDVE)
        SC.wait(cDVE, e_var)
        SC.do(lambda e, o=r2[:], i=r1[:]: e.activation(o, i, AF.Ln), cACT)
        e_exp = SC.do(lambda e, o=r1[:], i=r2[:]: e.activation(o, i, AF.Exp, scale=-0.5), cACT)
        VE.wait(cACT, e_exp)
        e_mr = VE.do(lambda e, o=r2[:], a=r0[:], b=r1[:]: e.tensor_tensor(o, a, b, ALU.mult), cDVE)
        return e_mr, e_exp

    def bcast(e_mr, e_exp, pe_waits):
        PE.waits(pe_waits)
        PE.wait(cDVE, e_mr)
        PE.wait(cACT, e_exp)
        PE.do(lambda e, o=PB0, l=ones1f[:], r=r1[:]: e.matmul(o, l, r, start=True, stop=True), cPE)
        g = PE.do(lambda e, o=PB1, l=ones1f[:], r=r2[:]: e.matmul(o, l, r, start=True, stop=True), cPE)
        VE.wait(cPE, g)
        VE.do(lambda e, o=Rb[:], i=PB0: e.tensor_copy(o, i), cDVE)
        e_c = VE.do(lambda e, o=MRb[:], i=PB1: e.tensor_copy(o, i), cDVE)
        return e_c, g

    e_xbf = [0] * nt
    e_xsq = [0] * nt
    e_ln1 = [0] * nt
    e_qstt = [[0] * DC for _ in range(nt)]
    e_kstt = [[0] * 4 for _ in range(nt)]
    e_vact = [[0] * 4 for _ in range(nt)]
    e_kcol = [0] * nt
    g_kv_ref = [0]
    g_kv_last = 0
    e_kvT = 0
    e_zlT = 0
    prologue_hooks = []
    v_qs_w = [0] * nt
    v_x = [0] * (nt + 1)
    paf = [(cDVE, 0)]
    pbf = [(cDVE, 0)]
    pcf = [(cDVE, 0)]

    v_x[0] = v_x0
    prep_sched = [(0, prep_w1), (2, prep_w2),
                  (3, lambda: prep_h1(0, 0)), (4, lambda: prep_h1(0, 1)),
                  (5, lambda: prep_h1(1, 0)), (6, lambda: prep_h1(1, 1)),
                  (7, lambda: prep_h1(2, 0))]


    def emit_casts(i):
        VE.wait(dX, v_x[i])
        e_xbf[i] = VE.do(lambda e, o=xbf[:], s=xslab[:]: e.tensor_copy(o, s), cDVE)
        SC.wait(dX, v_x[i])
        e_xsq[i] = SC.do(lambda e, o=xsq[:], s=xslab[:]: e.activation(o, s, AF.Square), cACT)

    v_q2 = [0] * (nt + 1)
    v_x2 = [0] * (nt + 1)
    v_out = [0] * nt
    e_z = [0] * (nt + 1)
    e_scoTT = [[0] * DC for _ in range(nt + 1)]
    e_asq_l = [0] * (nt + 1)
    e_yupd_l = [0] * (nt + 1)
    e_ysq = [0] * (nt + 1)
    e_ln2_l = [0] * (nt + 1)
    e_out_l = [0] * nt
    g_ao_l = [0] * (nt + 1)
    g_w1w2_l = [0] * nt
    rotb = [PA0, PA1, PD0, PD1]
    rot_free = [[(cACT, e_vact[nt - 1][2])], [(cACT, e_vact[nt - 1][2])],
                [(cDVE, e_kvT)], [(cDVE, e_kvT)]]
    rot_i = 0
    pc_free = [(cDVE, e_kcol[nt - 1])]
    pb_free = [(cACT, e_vact[nt - 1][3]), (cDVE, e_ln1[nt - 1])]
    slot_ap = [pan0[:, 0:2048], pan0[:, 2048:4096], pan1[:, 0:2048],
               pan1[:, 2048:4096], pan2[:, 0:2048], pan2[:, 2048:4096]]
    NSLOT = 6
    slot_free = [[(cDVE, 0)] for _ in range(NSLOT)]
    slot_v = [0] * NSLOT
    s_idx = 0

    def load_slot(src_ap, a_split, wait_extra):
        nonlocal s_idx
        s = s_idx % NSLOT
        s_idx += 1
        SY.waits(slot_free[s])
        SY.waits(wait_extra)
        dst = slot_ap[s].rearrange("p (a b) -> p a b", a=a_split)
        slot_v[s] = dma(SY, dst, src_ap, dS[s])
        return s

    def rot_next(waits=()):
        nonlocal rot_i
        b = rot_i % 4
        rot_i += 1
        PE.waits(rot_free[b])
        PE.waits(waits)
        return b


    def load_qx(j):
        if j == 0:
            SY.wait(dQS, 16 * (nt - 1))
        SY.wait(cPE, g_ao_l[j - 1] if j > 0 else g_kv_last)
        v_q2[j] = dma(SY, qt[:].rearrange("p (c n) -> p c n", c=DC),
                      qscr[j].rearrange("c p n -> p c n"), dQ)
        SY.wait(cDVE, e_yupd_l[j - 1] if j > 0 else e_ln1[nt - 1])
        v_x2[j] = dma(SY, Xsl[:].rearrange("p (c n) -> p c n", c=DC),
                      xtr[:, :, j * T:(j + 1) * T], dX)

    def attnA_hooks(j):
        hooks = []
        st_ = {}

        def h_zden():
            nonlocal pc_free
            PE.wait(dQ, v_q2[j])
            PE.wait(cDVE, e_zlT)
            PE.waits(pc_free)
            for c in range(DC):
                PE.do(lambda e, o=PC[0:16, 0:512], l=zlT[:, c * 16:(c + 1) * 16],
                      r=qt[:, c * T:(c + 1) * T], st=(c == 0), sp=(c == DC - 1):
                      e.matmul(o, l, r, start=st, stop=sp))
            g_zden = PE.inc_last(cPE)
            VE.wait(cPE, g_zden)
            st_["e_zread"] = VE.do(lambda e, o=su[0:16, 512:1024], p=PC[0:16, 0:512]:
                                   e.tensor_single_scalar(o, p, EPS_Z, ALU.add), cDVE)
            VE.do(lambda e, o=su[0:16, 1536:2048], i_=su[0:16, 512:1024]:
                  e.reciprocal(o, i_), cDVE)
            e_z[j] = VE.do(lambda e, o=zrb[:], i_=su[0:16, 1536:2048]: e.tensor_copy(o, i_), cDVE)
            st_["e_sv"] = [0] * DC
            st_["e_zbc"] = [0] * DC
        hooks.append(h_zden)

        def mk_sraw(c):
            def f():
                scb = PC0 if c % 2 == 0 else PC1
                if c == 0:
                    PE.wait(cDVE, st_["e_zread"])
                if c >= 2:
                    PE.wait(cACT, st_["e_sv"][c - 2])
                g_sr = PE.do(lambda e, o=scb, l=kvT[:, c * 128:(c + 1) * 128],
                             r=qt[:, c * T:(c + 1) * T]:
                             e.matmul(o, l, r, start=True, stop=True), cPE)
                SC.wait(cPE, g_sr)
                st_["e_sv"][c] = SC.do(lambda e, o=qt[:, c * T:(c + 1) * T], p=scb:
                                       e.activation(o, p, AF.Copy), cACT)
            return f
        for c in range(DC):
            hooks.append(mk_sraw(c))

        def mk_zb(c):
            def f():
                nonlocal pb_free
                zh = PB0 if c % 2 == 0 else PB1
                PE.wait(cDVE, e_z[j])
                if c < 2:
                    PE.waits(pb_free)
                else:
                    PE.wait(cDVE, st_["e_zbc"][c - 2])
                g_zb = PE.do(lambda e, o=zh, l=sel[:, c * 128:(c + 1) * 128], r=zrb[:]:
                             e.matmul(o, l, r, start=True, stop=True), cPE)
                VE.wait(cPE, g_zb)
                st_["e_zbc"][c] = VE.do(lambda e, o=zbs[:], p=zh: e.tensor_copy(o, p), cDVE)
                VE.wait(cACT, st_["e_sv"][c])
                e_scoTT[j][c] = VE.do(lambda e, o=qt[:, c * T:(c + 1) * T], b=zbs[:]:
                                      e.tensor_tensor(o, o, b, ALU.mult), cDVE)
            return f
        for c in range(DC):
            hooks.append(mk_zb(c))

        st_["e_asq"] = [0] * DC

        def mk_ao(oc):
            def f():
                half = PC0 if oc % 2 == 0 else PC1
                PE.wait(dWo, v_wo)
                PE.wait(cDVE, e_scoTT[j][DC - 1])
                if oc >= 2:
                    PE.wait(cACT, st_["e_asq"][oc - 2])
                for kc in range(DC):
                    PE.do(lambda e, o=half, l=wos[:, kc * D + oc * 128: kc * D + (oc + 1) * 128],
                          r=qt[:, kc * T:(kc + 1) * T], st=(kc == 0), sp=(kc == DC - 1):
                          e.matmul(o, l, r, start=st, stop=sp))
                g_ao = PE.inc_last(cPE)
                SC.wait(cPE, g_ao)
                if j == 0:
                    SC.wait(dQS, 16 * nt)
                else:
                    SC.wait(cDVE, e_yupd_l[j - 1])
                    SC.wait(cPE, g_w1w2_l[j - 1])
                SC.do(lambda e, o=aosl[:, oc * T:(oc + 1) * T], p=half,
                      b=bos[:, oc:oc + 1]: e.activation(o, p, AF.Identity, bias=b), cACT)
                st_["e_asq"][oc] = SC.do(lambda e, o=shsl[:, oc * T:(oc + 1) * T], p=half,
                                         b=bos[:, oc:oc + 1]:
                                         e.activation(o, p, AF.Square, bias=b), cACT)
                if oc == DC - 1:
                    g_ao_l[j] = cPE.n
                    e_asq_l[j] = st_["e_asq"][DC - 1]
            return f
        for oc in range(DC):
            hooks.append(mk_ao(oc))

        def h_lnattn():
            nonlocal pc_free, pb_free
            pc_free = [(cACT, e_asq_l[j])]
            PE.wait(cACT, e_asq_l[j])
            PE.wait(cDVE, st_["e_zbc"][DC - 1])
            for kc in range(DC):
                PE.do(lambda e, o=PB[0:1, 0:512], l=onesb[:], r=aosl[:, kc * T:(kc + 1) * T],
                      st=(kc == 0), sp=(kc == DC - 1): e.matmul(o, l, r, start=st, stop=sp))
            for kc in range(DC):
                PE.do(lambda e, o=PB[0:1, 512:1024], l=onesb[:], r=shsl[:, kc * T:(kc + 1) * T],
                      st=(kc == 0), sp=(kc == DC - 1): e.matmul(o, l, r, start=st, stop=sp))
            g_sa2 = PE.inc_last(cPE)
            st_["rows"] = rows(PB[0:1, 0:512], PB[0:1, 512:1024], g_sa2)
        hooks.append(h_lnattn)

        def h_ybc():
            nonlocal pb_free
            e_mr, e_exp = st_["rows"]
            e_bc, _ = bcast(e_mr, e_exp, [])
            Yb = Yb0 if j % 2 == 0 else Yb1
            VE.wait(cDVE, e_bc)
            VE.wait(dX, v_x2[j])
            if j >= 2:
                VE.wait(cDVE, e_out_l[j - 2])
            for c in range(DC):
                VE.do(lambda e, o=zbs[:], a=aosl[:, c * T:(c + 1) * T], b=Rb[:]:
                      e.tensor_tensor(o, a, b, ALU.mult), cDVE)
                VE.do(lambda e, o=su[:, 1024:1536], a=zbs[:], b=MRb[:]:
                      e.tensor_tensor(o, a, b, ALU.subtract), cDVE)
                VE.do(lambda e, o=Yb[:, c * T:(c + 1) * T],
                      a=Xsl[:, c * T:(c + 1) * T], b=su[:, 1024:1536]:
                      e.tensor_tensor(o, a, b, ALU.add), cDVE)
                VE.do(lambda e, o=shsl[:, c * T:(c + 1) * T], a=Yb[:, c * T:(c + 1) * T]:
                      e.tensor_tensor(o, a, a, ALU.mult), cDVE)
            e_yupd_l[j] = cDVE.n
            e_ysq[j] = cDVE.n
            pb_free = [(cDVE, e_yupd_l[j])]
        hooks.append(h_ybc)
        return hooks


    def emit_boundary():
        nonlocal g_kv_last, e_kvT, e_zlT
        g_kv_last = g_kv_ref[0]
        VE.wait(cPE, g_kv_last)
        VE.do(lambda e: e.memset(kvT[:], 0.0), cDVE)
        for c in range(DC):
            VE.do(lambda e, o=kvT[0:64, c * 128: c * 128 + 64],
                  p=PD[0:64, c * 128: c * 128 + 64]: e.tensor_copy(o, p), cDVE)
            VE.do(lambda e, o=kvT[64:128, c * 128 + 64: c * 128 + 128],
                  p=PD[64:128, c * 128 + 64: c * 128 + 128]: e.tensor_copy(o, p), cDVE)
        e_kvT = cDVE.n
        kcol_fin = kcol0 if (nt - 1) % 2 == 0 else kcol1
        for c in range(DC):
            VE.do(lambda e, o=zlT[:, c * 16:(c + 1) * 16], i_=ind[:, c * 16:(c + 1) * 16],
                  s=kcol_fin[:, c:c + 1]: e.tensor_scalar_mul(o, i_, s), cDVE)
        e_zlT = cDVE.n



    emit_casts(0)
    pend_kv = None
    for i in range(nt):
        PE.scope(f"p1t{i}_ln")
        PE.wait(cDVE, e_xbf[i])
        PE.wait(cACT, e_xsq[i])
        PE.waits(pbf)
        for kc in range(DC):
            PE.do(lambda e, o=PB[0:1, 0:512], l=onesb[:], r=xbf[:, kc * T:(kc + 1) * T],
                  st=(kc == 0), sp=(kc == DC - 1): e.matmul(o, l, r, start=st, stop=sp))
        g_sx = PE.inc_last(cPE)
        for kc in range(DC):
            PE.do(lambda e, o=PB[0:1, 512:1024], l=onesb[:], r=xsq[:, kc * T:(kc + 1) * T],
                  st=(kc == 0), sp=(kc == DC - 1): e.matmul(o, l, r, start=st, stop=sp))
        g_sx2 = PE.inc_last(cPE)
        if pend_kv is not None:
            pend_kv[0]()
        e_mr, e_exp = rows(PB[0:1, 0:512], PB[0:1, 512:1024], g_sx2)
        e_bc, g_bc = bcast(e_mr, e_exp, paf + pbf)
        pbf = [(cDVE, e_mr)]
        VE.wait(cDVE, e_bc)
        for kc in range(DC):
            VE.do(lambda e, o=zbs[:], a=xbf[:, kc * T:(kc + 1) * T], b=Rb[:]:
                  e.tensor_tensor(o, a, b, ALU.mult), cDVE)
            VE.do(lambda e, o=ln1[:, kc * T:(kc + 1) * T], a=zbs[:], b=MRb[:]:
                  e.tensor_tensor(o, a, b, ALU.subtract), cDVE)
        e_ln1[i] = cDVE.n
        paf = [(cDVE, e_ln1[i])]
        pbf = pbf + [(cDVE, e_ln1[i])]
        if i + 1 < nt:
            SY.wait(cDVE, e_xbf[i])
            SY.wait(cACT, e_xsq[i])
            v_x[i + 1] = dma(SY, xsl3, xtr[:, :, (i + 1) * T:(i + 2) * T], dX)
            emit_casts(i + 1)
        if pend_kv is not None:
            pend_kv[1]()
            pend_kv = None
        def emit_q(hooks=None, qb0=None, qb1=None):
            PE.scope(f"p1t{i}_q")
            PE.wait(cDVE, e_ln1[i])
            PE.wait(dWq, v_wq)
            PE.wait(dWrest, v_wrest)
            if i > 0:
                PE.wait(cDVE, e_kcol[i - 1])
            for oc in range(DC):
                half = (qb0 or PC0) if oc % 2 == 0 else (qb1 or PC1)
                if oc >= 2:
                    PE.wait(cDVE, e_qstt[i][oc - 2])
                if i == nt - 1:
                    PE.wait(cDVE, e_kcol[i])
                PE.do(lambda e, o=half, l=bqb[0:1, oc * 128:(oc + 1) * 128], r=ones512b[:]:
                      e.matmul(o, l, r, start=True, stop=False))
                for kc in range(DC):
                    PE.do(lambda e, o=half, l=wqs[:, kc * D + oc * 128: kc * D + (oc + 1) * 128],
                          r=ln1[:, kc * T:(kc + 1) * T], sp=(kc == DC - 1):
                          e.matmul(o, l, r, start=False, stop=sp))
                g_q = PE.inc_last(cPE)
                e_s = su[:, 0:512] if oc % 2 == 0 else su[:, 1024:1536]
                SC.wait(cPE, g_q)
                if i == nt - 1 and oc < 2:
                    SC.wait(cDVE, e_kstt[i][3])
                if oc >= 2:
                    SC.wait(cDVE, e_qstt[i][oc - 2])
                e_e = SC.do(lambda e, o=e_s, p=half: e.activation(o, p, AF.Exp), cACT)
                VE.wait(cACT, e_e)
                VE.do(lambda e, o=e_s: e.tensor_scalar_min(o, o, 1.0), cDVE)
                if i > 0:
                    VE.wait(dQS, v_qs_w[i - 1])
                e_qstt[i][oc] = VE.do(
                    lambda e, o=qslab[:, oc * T:(oc + 1) * T], p=half, m=e_s:
                    e.scalar_tensor_tensor(o, p, 0.0, m, ALU.max, ALU.add), cDVE)
                if hooks:
                    hooks.pop(0)()
                if hooks:
                    hooks.pop(0)()

        def emit_k():
            nonlocal paf, pbf
            PE.scope(f"p1t{i}_k")
            PE.wait(dWk, v_wk)
            for ns in range(4):
                bank = PA if ns % 2 == 0 else PB
                PE.waits(paf if ns % 2 == 0 else pbf)
                PE.do(lambda e, o=bank[:, 0:512], l=ones1b[:], r=bkb[0:1, 0:512]:
                      e.matmul(o, l, r, start=True, stop=False))
                PE.do(lambda e, o=bank[:, 512:1024], l=ones1b[:], r=bkb[0:1, 512:1024]:
                      e.matmul(o, l, r, start=True, stop=False))
                for kc in range(DC):
                    for h in range(2):
                        PE.do(lambda e, o=bank[:, h * 512:(h + 1) * 512],
                              l=ln1[:, kc * T + ns * 128: kc * T + (ns + 1) * 128],
                              r=wks[:, kc * D + h * 512: kc * D + (h + 1) * 512],
                              sp=(kc == DC - 1): e.matmul(o, l, r, start=False, stop=sp))
                g_k = PE.inc_last(cPE)
                eh = su[:, 0:1024] if ns % 2 == 0 else su[:, 1024:2048]
                SC.wait(cPE, g_k)
                SC.wait(cDVE, e_qstt[i][DC - 1])
                if ns >= 2:
                    SC.wait(cDVE, e_kstt[i][ns - 2])
                e_e = SC.do(lambda e, o=eh, p=bank[:]: e.activation(o, p, AF.Exp), cACT)
                VE.wait(cACT, e_e)
                VE.do(lambda e, o=eh: e.tensor_scalar_min(o, o, 1.0), cDVE)
                e_kstt[i][ns] = VE.do(
                    lambda e, o=kslab[:, ns * D:(ns + 1) * D], p=bank[:], m=eh:
                    e.scalar_tensor_tensor(o, p, 0.0, m, ALU.max, ALU.add), cDVE)
                if ns % 2 == 0:
                    paf = [(cACT, e_e), (cDVE, e_kstt[i][ns])]
                else:
                    pbf = [(cACT, e_e), (cDVE, e_kstt[i][ns])]

        def emit_v():
            nonlocal paf, pbf
            PE.scope(f"p1t{i}_v")
            PE.wait(dWv, v_wv)
            for ns in range(4):
                bank = PA if ns % 2 == 0 else PB
                PE.waits(paf if ns % 2 == 0 else pbf)
                PE.do(lambda e, o=bank[:, 0:512], l=ones1b[:], r=bvb[0:1, 0:512]:
                      e.matmul(o, l, r, start=True, stop=False))
                PE.do(lambda e, o=bank[:, 512:1024], l=ones1b[:], r=bvb[0:1, 512:1024]:
                      e.matmul(o, l, r, start=True, stop=False))
                for kc in range(DC):
                    for h in range(2):
                        PE.do(lambda e, o=bank[:, h * 512:(h + 1) * 512],
                              l=ln1[:, kc * T + ns * 128: kc * T + (ns + 1) * 128],
                              r=wvs[:, kc * D + h * 512: kc * D + (h + 1) * 512],
                              sp=(kc == DC - 1): e.matmul(o, l, r, start=False, stop=sp))
                g_v = PE.inc_last(cPE)
                SC.wait(cPE, g_v)
                e_vact[i][ns] = SC.do(lambda e, o=vslab[:, ns * D:(ns + 1) * D], p=bank[:]:
                                      e.activation(o, p, AF.Copy), cACT)
                if ns % 2 == 0:
                    paf = [(cACT, e_vact[i][ns])]
                else:
                    pbf = [(cACT, e_vact[i][ns])]

        if i < nt - 1:
            emit_q()
            emit_k()
            emit_v()
        else:
            emit_k()
            emit_v()
        def mk_kv(i):
            def kv_part():
                PE.scope(f"p1t{i}_kv")
                for ns in range(4):
                    PE.wait(cDVE, e_kstt[i][ns])
                    PE.wait(cACT, e_vact[i][ns])
                    for c in range(DC):
                        PE.do(lambda e, o=PD[:, c * 128:(c + 1) * 128],
                              l=kslab[:, ns * D + c * 128: ns * D + (c + 1) * 128],
                              r=vslab[:, ns * D + c * 128: ns * D + (c + 1) * 128],
                              st=(i == 0 and ns == 0 and c % 4 == 0),
                              sp=(i == nt - 1 and ns == 3 and c % 4 == 3):
                              e.matmul(o, l, r, start=st, stop=sp))
                nonlocal_g = PE.inc_last(cPE)
                g_kv_ref[0] = nonlocal_g

            def ks_part():
                PE.scope(f"p1t{i}_ks")
                if i == nt - 1 and i > 0:
                    PE.wait(cDVE, max(e_qstt[i - 1][DC - 1], e_kcol[i - 1]))
                else:
                    PE.wait(cDVE, e_qstt[i][DC - 1])
                for ns in range(4):
                    for c in range(DC):
                        PE.do(lambda e, o=PC[:, 512 + c:513 + c],
                              l=kslab[:, ns * D + c * 128: ns * D + (c + 1) * 128],
                              r=onesb[:], st=(ns == 0 and c == 0), sp=(ns == 3 and c == DC - 1):
                              e.matmul(o, l, r, start=st, stop=sp))
                g_ks = PE.inc_last(cPE)
                VE.wait(cPE, g_ks)
                if i == 0:
                    e_kcol[i] = VE.do(lambda e, o=kcol0[:], p=PC[:, 512:520]: e.tensor_copy(o, p), cDVE)
                else:
                    sr = kcol0 if (i - 1) % 2 == 0 else kcol1
                    ds = kcol0 if i % 2 == 0 else kcol1
                    e_kcol[i] = VE.do(lambda e, o=ds[:], a=sr[:], b=PC[:, 512:520]:
                                      e.tensor_tensor(o, a, b, ALU.add), cDVE)
            return (kv_part, ks_part)

        pend_kv = mk_kv(i)
        if i == nt - 1:
            pend_kv[0]()
            pend_kv[1]()
            pend_kv = None
            emit_boundary()
            load_qx(0)
            pc_free = [(cDVE, e_kcol[i])]
            pb_free = [(cACT, e_vact[i][3]), (cDVE, e_ln1[i])]
            hooks0 = attnA_hooks(0)
            emit_q(hooks0, qb0=PA0, qb1=PA1)
            prologue_hooks.extend(hooks0)
        pcf_prev_i = i
        SY.wait(cDVE, e_qstt[i][DC - 1])
        v_qs_w[i] = dma(SY, qscr[i].rearrange("c p n -> p c n"),
                        qslab[:].rearrange("p (c n) -> p c n", c=DC), dQS)
        for when, fn in prep_sched:
            if fn is not None and min(when, nt - 1) == i:
                if i == 0:
                    GP.wait(cDVE, e_xbf[0])
                elif i > 0:
                    GP.wait(cDVE, e_kcol[i - 1])
                fn()
        prep_sched = [(w, None if min(w, nt - 1) <= i else f) for w, f in prep_sched]

    def attnB_hooks(j):
        st_ = {}

        def h_stats():
            nonlocal pb_free
            Yb = Yb0 if j % 2 == 0 else Yb1
            PE.wait(cDVE, e_ysq[j])
            PE.waits(pb_free)
            for kc in range(DC):
                PE.do(lambda e, o=PB[0:1, 0:512], l=onesb[:], r=Yb[:, kc * T:(kc + 1) * T],
                      st=(kc == 0), sp=(kc == DC - 1): e.matmul(o, l, r, start=st, stop=sp))
            for kc in range(DC):
                PE.do(lambda e, o=PB[0:1, 512:1024], l=onesb[:], r=shsl[:, kc * T:(kc + 1) * T],
                      st=(kc == 0), sp=(kc == DC - 1): e.matmul(o, l, r, start=st, stop=sp))
            st_["g"] = PE.inc_last(cPE)
            st_["rows"] = rows(PB[0:1, 0:512], PB[0:1, 512:1024], st_["g"])

        def h_apply():
            nonlocal pb_free
            Yb = Yb0 if j % 2 == 0 else Yb1
            e_mr2, e_exp2 = st_["rows"]
            e_bc2, _ = bcast(e_mr2, e_exp2, [])
            VE.wait(cDVE, e_bc2)
            VE.wait(cPE, st_["g"])
            for c in range(DC):
                VE.do(lambda e, o=zbs[:], a=Yb[:, c * T:(c + 1) * T], b=Rb[:]:
                      e.tensor_tensor(o, a, b, ALU.mult), cDVE)
                VE.do(lambda e, o=shsl[:, c * T:(c + 1) * T], a=zbs[:], b=MRb[:]:
                      e.tensor_tensor(o, a, b, ALU.subtract), cDVE)
            e_ln2_l[j] = cDVE.n
            pb_free = [(cDVE, e_ln2_l[j])]
        return [h_stats, h_apply]

    rot_free[0] = [(cDVE, e_qstt[nt - 1][DC - 2])]
    rot_free[1] = [(cDVE, e_qstt[nt - 1][DC - 1])]
    rot_free[2] = [(cDVE, e_kvT)]
    rot_free[3] = [(cDVE, e_kvT)]
    PE.scope("prologue")
    while prologue_hooks:
        prologue_hooks.pop(0)()
    for h_ in attnB_hooks(0):
        h_()
    GP.wait(cDVE, e_yupd_l[0])
    prep_h1(2, 1)
    prep_h1(3)

    for i in range(nt):
        Ybi = Yb0 if i % 2 == 0 else Yb1
        if i + 1 < nt:
            load_qx(i + 1)
        PE.scope(f"r{i}_w1w2")
        e_h = [0] * HC
        g_u = g_w = 0
        for g8 in range(16):
            s1_ = load_slot(w1b[g8 * 2:(g8 + 1) * 2].rearrange("a p b c -> p a (b c)"),
                            2, [(dPw1, v_pw1)])
            s2_ = load_slot(w2b[g8 * 2:(g8 + 1) * 2].rearrange("a p b c -> p a (b c)"),
                            2, [(dPw2, v_pw2)])
            for jj in range(2):
                oc = g8 * 2 + jj
                ru = rot_next([(dS[s1_], slot_v[s1_]), (cDVE, e_ln2_l[i])])
                for kc in range(DC):
                    PE.do(lambda e, o=rotb[ru], p=slot_ap[s1_],
                          lo=jj * 1024 + kc * 128, r=shsl[:, kc * T:(kc + 1) * T],
                          st=(kc == 0), sp=(kc == DC - 1):
                          e.matmul(o, p[:, lo:lo + 128], r, start=st, stop=sp))
                g_u = PE.inc_last(cPE)
                rw = rot_next([(dS[s2_], slot_v[s2_])])
                for kc in range(DC):
                    PE.do(lambda e, o=rotb[rw], p=slot_ap[s2_],
                          lo=jj * 1024 + kc * 128, r=shsl[:, kc * T:(kc + 1) * T],
                          st=(kc == 0), sp=(kc == DC - 1):
                          e.matmul(o, p[:, lo:lo + 128], r, start=st, stop=sp))
                g_w = PE.inc_last(cPE)
                sh_ = su[:, 0:512] if oc % 2 == 0 else su[:, 512:1024]
                SC.wait(cPE, g_u)
                if oc >= 2:
                    SC.wait(cDVE, e_h[oc - 2])
                e_si = SC.do(lambda e, o=sh_, p=rotb[ru], s=s1s[:, oc:oc + 1]:
                             e.activation(o, p, AF.Silu, scale=s), cACT)
                VE.wait(cACT, e_si)
                VE.wait(cPE, g_w)
                e_h[oc] = VE.do(lambda e, o=hsl[:, oc * T:(oc + 1) * T], p=rotb[rw],
                                s=s2s[:, oc:oc + 1], m=sh_:
                                e.scalar_tensor_tensor(o, p, s, m, ALU.mult, ALU.mult), cDVE)
                rot_free[ru] = [(cACT, e_si)]
                rot_free[rw] = [(cDVE, e_h[oc])]
            slot_free[s1_] = [(cPE, g_u)]
            slot_free[s2_] = [(cPE, g_w)]
        PE.scope(f"r{i}_h1")
        g_w1w2_l[i] = g_w
        e_g = [0] * HC
        for oc in range(HC):
            sa = load_slot(h1b[oc][:, 0:16, :], 16, [(dPh1, v_ph1)])
            sb = load_slot(h1b[oc][:, 16:32, :], 16, [(dPh1, v_ph1)])
            r_ = rot_next([(dS[sa], slot_v[sa]), (dS[sb], slot_v[sb]),
                           (cDVE, e_h[HC - 1])])
            for kc in range(HC):
                sl = slot_ap[sa] if kc < 16 else slot_ap[sb]
                PE.do(lambda e, o=rotb[r_], p=sl, lo=(kc % 16) * 128,
                      r=hsl[:, kc * T:(kc + 1) * T], st=(kc == 0), sp=(kc == HC - 1):
                      e.matmul(o, p[:, lo:lo + 128], r, start=st, stop=sp))
            g_ = PE.inc_last(cPE)
            SC.wait(cPE, g_)
            if i > 0 and oc < 16:
                SC.wait(cDVE, e_out_l[i - 1])
            if i > 0 and oc == 16:
                SC.wait(dOut, v_out[i - 1])
            e_g[oc] = SC.do(lambda e, o=gsl[:, oc * T:(oc + 1) * T], p=rotb[r_],
                            s=sh1s[:, oc:oc + 1]: e.activation(o, p, AF.Gelu, scale=s), cACT)
            rot_free[r_] = [(cACT, e_g[oc])]
            slot_free[sa] = [(cPE, g_)]
            slot_free[sb] = [(cPE, g_)]
            if i == 0 and oc in (2, 6, 10, 14, 18, 22, 26, 30):
                GP.wait(cPE, g_)
                prep_h2(oc // 8, (oc % 8) // 4)
        hooksA = attnA_hooks(i + 1) if i + 1 < nt else []
        PE.scope(f"r{i}_h2")
        e_h2 = [0] * HC
        for oc in range(HC):
            if oc >= 2 and hooksA:
                hooksA.pop(0)()
            if oc >= 2 and hooksA:
                hooksA.pop(0)()
            sa = load_slot(h2b[oc][:, 0:16, :], 16, [(dPh2, v_ph2)])
            sb = load_slot(h2b[oc][:, 16:32, :], 16, [(dPh2, v_ph2)])
            r_ = rot_next([(dS[sa], slot_v[sa]), (dS[sb], slot_v[sb]),
                           (cACT, e_g[HC - 1])])
            for kc in range(HC):
                sl = slot_ap[sa] if kc < 16 else slot_ap[sb]
                PE.do(lambda e, o=rotb[r_], p=sl, lo=(kc % 16) * 128,
                      r=gsl[:, kc * T:(kc + 1) * T], st=(kc == 0), sp=(kc == HC - 1):
                      e.matmul(o, p[:, lo:lo + 128], r, start=st, stop=sp))
            g_ = PE.inc_last(cPE)
            SC.wait(cPE, g_)
            e_h2[oc] = SC.do(lambda e, o=hsl[:, oc * T:(oc + 1) * T], p=rotb[r_],
                             s=sh2s[:, oc:oc + 1]: e.activation(o, p, AF.Gelu, scale=s), cACT)
            rot_free[r_] = [(cACT, e_h2[oc])]
            slot_free[sa] = [(cPE, g_)]
            slot_free[sb] = [(cPE, g_)]
            if i == 0 and oc in (2, 10):
                GP.wait(cPE, g_)
                prep_w3(oc // 8)
        while hooksA:
            hooksA.pop(0)()
        hooksB = attnB_hooks(i + 1) if i + 1 < nt else []
        PE.scope(f"r{i}_w3")
        e_fsq = [0] * DC
        g_w3 = [0] * DC
        for oc in range(DC):
            if oc in (0, 3) and hooksB:
                hooksB.pop(0)()
            sa = load_slot(w3b[oc][:, 0:16, :], 16, [(dPw3, v_pw3)])
            sb = load_slot(w3b[oc][:, 16:32, :], 16, [(dPw3, v_pw3)])
            r_ = rot_next([(dS[sa], slot_v[sa]), (dS[sb], slot_v[sb]),
                           (cACT, e_h2[HC - 1])])
            for kc in range(HC):
                sl = slot_ap[sa] if kc < 16 else slot_ap[sb]
                PE.do(lambda e, o=rotb[r_], p=sl, lo=(kc % 16) * 128,
                      r=hsl[:, kc * T:(kc + 1) * T], st=(kc == 0), sp=(kc == HC - 1):
                      e.matmul(o, p[:, lo:lo + 128], r, start=st, stop=sp))
            g_w3[oc] = PE.inc_last(cPE)
            SC.wait(cPE, g_w3[oc])
            SC.do(lambda e, o=gsl[:, oc * T:(oc + 1) * T], p=rotb[r_],
                  s=s3s[:, oc:oc + 1]: e.activation(o, p, AF.Identity, scale=s), cACT)
            e_fsq[oc] = SC.do(lambda e, o=gsl[:, HID + oc * T: HID + (oc + 1) * T],
                              p=rotb[r_], s=s3s[:, oc:oc + 1]:
                              e.activation(o, p, AF.Square, scale=s), cACT)
            rot_free[r_] = [(cACT, e_fsq[oc])]
            slot_free[sa] = [(cPE, g_w3[oc])]
            slot_free[sb] = [(cPE, g_w3[oc])]
        while hooksB:
            hooksB.pop(0)()
        while hooksB:
            hooksB.pop(0)()
        PE.scope(f"r{i}_lnf")
        PE.wait(cACT, e_fsq[DC - 1])
        PE.waits(pb_free)
        for kc in range(DC):
            PE.do(lambda e, o=PB[0:1, 0:512], l=onesb[:], r=gsl[:, kc * T:(kc + 1) * T],
                  st=(kc == 0), sp=(kc == DC - 1): e.matmul(o, l, r, start=st, stop=sp))
        for kc in range(DC):
            PE.do(lambda e, o=PB[0:1, 512:1024], l=onesb[:],
                  r=gsl[:, HID + kc * T: HID + (kc + 1) * T],
                  st=(kc == 0), sp=(kc == DC - 1): e.matmul(o, l, r, start=st, stop=sp))
        g_sf2 = PE.inc_last(cPE)
        e_mr3, e_exp3 = rows(PB[0:1, 0:512], PB[0:1, 512:1024], g_sf2)
        e_bc3, _ = bcast(e_mr3, e_exp3, [])
        VE.wait(cDVE, e_bc3)
        VE.wait(cPE, g_w3[DC - 1])
        for c in range(DC):
            VE.do(lambda e, o=zbs[:], a=gsl[:, c * T:(c + 1) * T], b=Rb[:]:
                  e.tensor_tensor(o, a, b, ALU.mult), cDVE)
            VE.do(lambda e, o=zbs2[:], a=zbs[:], b=MRb[:]:
                  e.tensor_tensor(o, a, b, ALU.subtract), cDVE)
            VE.do(lambda e, o=outsl[:, c * T:(c + 1) * T],
                  a=Ybi[:, c * T:(c + 1) * T], b=zbs2[:]:
                  e.tensor_tensor(o, a, b, ALU.add), cDVE)
        e_out_l[i] = cDVE.n
        pb_free = [(cDVE, e_out_l[i])]
        SY.wait(cDVE, e_out_l[i])
        v_out[i] = dma(SY, ytr[:, :, i * T:(i + 1) * T],
                       outsl[:].rearrange("p (c n) -> p c n", c=DC), dOut)

    SY.wait(dOut, v_out[nt - 1])

    _simulate([SY, GP, PE, SC, VE], tag=f"nt={nt}")

    def replay(rec, self_sync=False):
        def f(eng):
            last_inc = None
            open_scope = [None]
            for op in rec.ops:
                if op[0] == "s":
                    if open_scope[0] is not None:
                        nc.leave_named_scope(open_scope[0][0], open_scope[0][1], False)
                        open_scope[0] = None
                    if op[1] is not None:
                        sid, _ = nc.enter_named_scope(op[1], False)
                        open_scope[0] = (op[1], sid)
                elif op[0] == "w":
                    eng.wait_ge(op[1].h, op[2])
                else:
                    if self_sync and last_inc is not None:
                        eng.wait_ge(last_inc[0].h, last_inc[1])
                    ins = op[1](eng)
                    if op[2] is not None:
                        ins.then_inc(op[2].h, op[3])
                        op[2].n_replay += op[3]
                        if op[3] == 1:
                            last_inc = (op[2], op[2].n_replay)
            if open_scope[0] is not None:
                nc.leave_named_scope(open_scope[0][0], open_scope[0][1], False)
        return f

    for s_ in all_sems:
        s_.n_replay = 0

    with nc.Block() as block:
        block.sync(replay(SY))
        block.gpsimd(replay(GP))
        block.tensor(replay(PE))
        block.scalar(replay(SC, self_sync=paranoid))
        block.vector(replay(VE, self_sync=paranoid))
    _sem_ctx.close()
    return nc


def _host_layout(inputs, nt=NT):
    x = np.asarray(inputs["L0_attn_x"], dtype=np.float32)
    B = x.shape[0]
    g = lambda k: np.asarray(inputs[k], dtype=np.float32)
    C = np.ascontiguousarray
    tile_kxo = lambda wT, kc, oc: C(wT.reshape(kc, 128, oc, 128).transpose(2, 1, 0, 3))
    common = {
        "wq": C(g("L0_attn_wq")), "wk": C(g("L0_attn_wk")),
        "wv": C(g("L0_attn_wv")), "wo": C(g("L0_attn_wo")),
        "bqr": C(g("L0_attn_bq").reshape(1, D)),
        "bot": C(g("L0_attn_bo").reshape(DC, 128).T),
        "bk": C(g("L0_attn_bk").reshape(1, D)),
        "bv": C(g("L0_attn_bv").reshape(1, D)),
        "w1t": tile_kxo(g("L0_ffn_w1_w").T, DC, HC),
        "w2t": tile_kxo(g("L0_ffn_w2_w").T, DC, HC),
        "h1t": tile_kxo(g("L0_ffn_h1_w").T, HC, HC),
        "h2t": tile_kxo(g("L0_ffn_h2_w").T, HC, HC),
        "w3t": tile_kxo(g("L0_ffn_w3_w").T, HC, DC),
        "s1": C(g("L0_ffn_w1_s").reshape(HC, 128).T),
        "s2": C(g("L0_ffn_w2_s").reshape(HC, 128).T),
        "sh1": C(g("L0_ffn_h1_s").reshape(HC, 128).T),
        "sh2": C(g("L0_ffn_h2_s").reshape(HC, 128).T),
        "s3": C(g("L0_ffn_w3_s").reshape(DC, 128).T),
    }
    selm = np.zeros((16, DC * 128), dtype=np.float32)
    indm = np.zeros((128, DC * 16), dtype=np.float32)
    for c in range(DC):
        selm[2 * c, c * 128: c * 128 + 64] = 1.0
        selm[2 * c + 1, c * 128 + 64: c * 128 + 128] = 1.0
        indm[0:64, c * 16 + 2 * c] = 1.0
        indm[64:128, c * 16 + 2 * c + 1] = 1.0
    common["selc"] = selm
    common["indc"] = indm

    in_maps = []
    for ci in range(B):
        m = dict(common)
        m["xt"] = C(x[ci, : nt * T].T)
        in_maps.append(m)
    return in_maps


def kernel(**inputs):
    nc = build(NT)
    in_maps = _host_layout(inputs, NT)
    res = run_bass_kernel_spmd(nc, in_maps, core_ids=list(range(N_CORES)))
    out = np.stack([np.ascontiguousarray(np.asarray(r["yt"], dtype=np.float32).T)
                    for r in res.results], axis=0)
    return out


# version 18
# speedup vs baseline: 1.0328x; 1.0328x over previous
import numpy as np
import concourse.bass as bass
import concourse.mybir as mybir
from concourse.bass_utils import run_bass_kernel_spmd

F32 = mybir.dt.float32
BF16 = mybir.dt.bfloat16
AF = mybir.ActivationFunctionType
ALU = mybir.AluOpType

D = 1024
DC = 8
HID = 4096
HC = 32
T = 512
NT = 8
N_CORES = 8
EPS_LN = 1e-5
EPS_Z = 1e-6


class Sem:
    def __init__(self, handle, name):
        self.h, self.name, self.n = handle, name, 0
        self.n_replay = 0


class Rec:

    def __init__(self, name):
        self.name = name
        self.ops = []
        self._lastw = {}

    def wait(self, sem, val):
        if val <= 0:
            return
        if self._lastw.get(sem.name, -1) >= val:
            return
        self._lastw[sem.name] = val
        import sys as _s
        self.ops.append(("w", sem, val,
                         (_s._getframe(1).f_lineno, _s._getframe(2).f_lineno if _s._getframe(1).f_back else 0)))

    def waits(self, lst):
        for s, v in lst:
            self.wait(s, v)

    def do(self, fn, sem=None, amt=1):
        import sys as _s
        self.ops.append(("e", fn, sem, amt,
                         (_s._getframe(1).f_lineno, _s._getframe(2).f_lineno if _s._getframe(1).f_back else 0)))
        if sem is not None:
            sem.n += amt
            return sem.n
        return None

    def scope(self, name):
        self.ops.append(("s", name, None))

    def inc_last(self, sem, amt=1):
        op = self.ops[-1]
        assert op[0] == "e" and op[2] is None
        self.ops[-1] = ("e", op[1], sem, amt)
        sem.n += amt
        return sem.n


def _simulate(recs, tag=""):
    vals = {}
    pcs = [0] * len(recs)
    progress = True
    while progress:
        progress = False
        for i, r in enumerate(recs):
            while pcs[i] < len(r.ops):
                op = r.ops[pcs[i]]
                if op[0] == "w":
                    if vals.get(op[1].name, 0) < op[2]:
                        break
                else:
                    if op[2] is not None:
                        vals[op[2].name] = vals.get(op[2].name, 0) + op[3]
                pcs[i] += 1
                progress = True
    stuck = {}
    for i, r in enumerate(recs):
        if pcs[i] < len(r.ops):
            op = r.ops[pcs[i]]
            stuck[r.name] = (pcs[i], len(r.ops),
                             op[1].name if op[0] == "w" else None,
                             op[2] if op[0] == "w" else None,
                             op[3] if op[0] == "w" and len(op) > 3 else None)
    if stuck:
        for i, r in enumerate(recs):
            if pcs[i] < len(r.ops):
                print(f"--- {r.name} pc {pcs[i]}: context:")
                for k in range(max(0, pcs[i] - 3), min(len(r.ops), pcs[i] + 2)):
                    op = r.ops[k]
                    print("   ", k, op[0], op[1] if op[0] != "e" else "",
                          op[2] if op[0] == "w" else "", op[-1] if isinstance(op[-1], tuple) else "")
    assert not stuck, f"DEADLOCK {tag}: {stuck}; sems={vals}"


def build(nt=NT, paranoid=False):
    nc = bass.Bass("TRN2", target_bir_lowering=False, debug=False,
                   detect_race_conditions=paranoid)
    NTOK = nt * T

    ap = lambda h: h.ap()
    xt = ap(nc.dram_tensor("xt", [D, NTOK], F32, kind="ExternalInput"))
    wq = ap(nc.dram_tensor("wq", [D, D], F32, kind="ExternalInput"))
    wk = ap(nc.dram_tensor("wk", [D, D], F32, kind="ExternalInput"))
    wv = ap(nc.dram_tensor("wv", [D, D], F32, kind="ExternalInput"))
    wo = ap(nc.dram_tensor("wo", [D, D], F32, kind="ExternalInput"))
    bqr = ap(nc.dram_tensor("bqr", [1, D], F32, kind="ExternalInput"))
    bot = ap(nc.dram_tensor("bot", [128, DC], F32, kind="ExternalInput"))
    bk = ap(nc.dram_tensor("bk", [1, D], F32, kind="ExternalInput"))
    bv = ap(nc.dram_tensor("bv", [1, D], F32, kind="ExternalInput"))
    w1t = ap(nc.dram_tensor("w1t", [HC, 128, DC, 128], F32, kind="ExternalInput"))
    w2t = ap(nc.dram_tensor("w2t", [HC, 128, DC, 128], F32, kind="ExternalInput"))
    h1t = ap(nc.dram_tensor("h1t", [HC, 128, HC, 128], F32, kind="ExternalInput"))
    h2t = ap(nc.dram_tensor("h2t", [HC, 128, HC, 128], F32, kind="ExternalInput"))
    w3t = ap(nc.dram_tensor("w3t", [DC, 128, HC, 128], F32, kind="ExternalInput"))
    s1 = ap(nc.dram_tensor("s1", [128, HC], F32, kind="ExternalInput"))
    s2 = ap(nc.dram_tensor("s2", [128, HC], F32, kind="ExternalInput"))
    sh1 = ap(nc.dram_tensor("sh1", [128, HC], F32, kind="ExternalInput"))
    sh2 = ap(nc.dram_tensor("sh2", [128, HC], F32, kind="ExternalInput"))
    s3 = ap(nc.dram_tensor("s3", [128, DC], F32, kind="ExternalInput"))
    selc = ap(nc.dram_tensor("selc", [16, DC * 128], F32, kind="ExternalInput"))
    indc = ap(nc.dram_tensor("indc", [128, DC * 16], F32, kind="ExternalInput"))
    yt = ap(nc.dram_tensor("yt", [D, NTOK], F32, kind="ExternalOutput"))

    w1b = ap(nc.dram_tensor("w1b", [HC, 128, DC, 128], BF16, kind="Internal"))
    w2b = ap(nc.dram_tensor("w2b", [HC, 128, DC, 128], BF16, kind="Internal"))
    h1b = ap(nc.dram_tensor("h1b", [HC, 128, HC, 128], BF16, kind="Internal"))
    h2b = ap(nc.dram_tensor("h2b", [HC, 128, HC, 128], BF16, kind="Internal"))
    w3b = ap(nc.dram_tensor("w3b", [DC, 128, HC, 128], BF16, kind="Internal"))
    qscr = ap(nc.dram_tensor("qscr", [nt, DC, 128, T], BF16, kind="Internal"))

    xtr = xt.rearrange("(c p) n -> p c n", p=128)
    ytr = yt.rearrange("(c p) n -> p c n", p=128)

    A = lambda name, shape, dt: nc.alloc_sbuf_tensor(name, shape, dt)
    per_part_total = nc.sbuf_bytes_remaining
    pan0 = A("pan0", [128, 4096], BF16)
    pan1 = A("pan1", [128, 4096], BF16)
    pan2 = A("pan2", [128, 4096], BF16)
    su = A("su", [128, 2048], F32)
    r0 = A("r0", [1, T], F32)
    r1 = A("r1", [1, T], F32)
    r2 = A("r2", [1, T], F32)
    zrb = A("zrb", [16, T], BF16)
    zbs = A("zbs", [128, T], BF16)
    zbs2 = A("zbs2", [128, T], BF16)
    Rb = A("Rb", [128, T], BF16)
    MRb = A("MRb", [128, T], BF16)
    kvT = A("kvT", [128, DC * 128], BF16)
    kcol0 = A("kcol0", [128, DC], F32)
    kcol1 = A("kcol1", [128, DC], F32)
    sel = A("sel", [16, DC * 128], BF16)
    ind = A("ind", [128, DC * 16], BF16)
    zlT = A("zlT", [128, DC * 16], BF16)
    bos = A("bos", [128, DC], F32)
    s1s = A("s1s", [128, HC], F32)
    s2s = A("s2s", [128, HC], F32)
    sh1s = A("sh1s", [128, HC], F32)
    sh2s = A("sh2s", [128, HC], F32)
    s3s = A("s3s", [128, DC], F32)
    onesb = A("onesb", [128, 1], BF16)
    ones1f = A("ones1f", [1, 128], F32)
    ones1b = A("ones1b", [1, 128], BF16)
    wos = A("wos", [128, DC * D], BF16)

    auto_used = per_part_total - nc.sbuf_bytes_remaining
    manual_need = 122880
    mbase = (per_part_total - manual_need) & ~31
    assert auto_used + 16512 <= mbase, \
        f"SBUF overflow: auto {auto_used}+16512 > base {mbase}"
    M = lambda name, shape, dt, off: nc.alloc_sbuf_tensor_at(name, shape, dt, offset=mbase + off)
    wqs = M("wqs", [128, DC * D], BF16, 0)
    wks = M("wks", [128, DC * D], BF16, 16384)
    wvs = M("wvs", [128, DC * D], BF16, 32768)
    xslab = M("xslab", [128, DC * T], F32, 49152)
    xbf = M("xbf", [128, DC * T], BF16, 65536)
    xsq = M("xsq", [128, DC * T], BF16, 73728)
    ln1 = M("ln1", [128, DC * T], BF16, 81920)
    kslab = M("kslab", [128, 4 * D], BF16, 90112)
    vslab = M("vslab", [128, 4 * D], BF16, 98304)
    qslab = M("qslab", [128, DC * T], BF16, 106496)
    hsl = M("hsl", [128, HC * T], BF16, 0)
    outsl = M("outsl", [128, DC * T], F32, 49152)
    gsl = M("gsl", [128, HC * T], BF16, 32768)
    Xsl = M("Xsl", [128, DC * T], F32, 65536)
    Yb0 = M("Yb0", [128, DC * T], BF16, 81920)
    Yb1 = M("Yb1", [128, DC * T], BF16, 90112)
    qt = M("qt", [128, DC * T], BF16, 98304)
    shsl = M("shsl", [128, DC * T], BF16, 106496)
    aosl = M("aosl", [128, DC * T], BF16, 114688)
    bqb = M("bqb", [1, D], BF16, 114688)
    bkb = M("bkb", [1, D], BF16, 116736)
    bvb = M("bvb", [1, D], BF16, 118784)
    ones512b = M("ones512b", [1, T], BF16, 120832)

    PA = nc.alloc_psum_tensor("PA", [128, 1024], F32)
    PB = nc.alloc_psum_tensor("PB", [128, 1024], F32)
    PC = nc.alloc_psum_tensor("PC", [128, 1024], F32)
    PD = nc.alloc_psum_tensor("PD", [128, 1024], F32)
    PA0, PA1 = PA[:, 0:512], PA[:, 512:1024]
    PB0, PB1 = PB[:, 0:512], PB[:, 512:1024]
    PC0, PC1 = PC[:, 0:512], PC[:, 512:1024]
    PD0, PD1 = PD[:, 0:512], PD[:, 512:1024]

    from contextlib import ExitStack
    _sem_ctx = ExitStack()
    mks = lambda name: Sem(_sem_ctx.enter_context(nc.semaphore(name)), name)
    cPE, cACT, cDVE = mks("cPE"), mks("cACT"), mks("cDVE")
    dAux = mks("dAux")
    dWrest, dWq, dWk, dWv, dWo = mks("dWrest"), mks("dWq"), mks("dWk"), mks("dWv"), mks("dWo")
    dPw1, dPw2, dPh1, dPh2, dPw3 = mks("dPw1"), mks("dPw2"), mks("dPh1"), mks("dPh2"), mks("dPw3")
    dS = [mks(f"dS{i}") for i in range(6)]
    dX, dQ, dQS, dOut = mks("dX"), mks("dQ"), mks("dQS"), mks("dOut")
    all_sems = [cPE, cACT, cDVE, dAux, dWrest, dWq, dWk, dWv, dWo,
                dPw1, dPw2, dPh1, dPh2, dPw3] + dS + [dX, dQ, dQS, dOut]

    SY, GP, PE, SC, VE = Rec("sync"), Rec("gpsimd"), Rec("tensor"), Rec("scalar"), Rec("vector")

    def dma(rec, out_ap, in_ap, sem):
        return rec.do(lambda e, o=out_ap, i=in_ap: e.dma_start(out=o, in_=i), sem, 16)

    dma(SC, bos[:], bot, dAux)
    dma(SC, s1s[:], s1, dAux)
    dma(SC, s2s[:], s2, dAux)
    dma(SC, sh1s[:], sh1, dAux)
    dma(SC, sh2s[:], sh2, dAux)
    dma(SC, s3s[:], s3, dAux)
    v_aux = dAux.n
    SC.wait(dAux, v_aux)

    dma(GP, sel[:], selc, dWrest)
    dma(GP, ind[:], indc, dWrest)
    dma(GP, bqb[:], bqr, dWrest)
    dma(GP, bkb[:], bk, dWrest)
    dma(GP, bvb[:], bv, dWrest)
    v_wrest = dWrest.n
    xsl3 = xslab[:].rearrange("p (c n) -> p c n", c=DC)
    v_x0 = dma(SY, xsl3, xtr[:, :, 0:T], dX)

    wq_r = wq.rearrange("(kc kp) o -> kp kc o", kp=128)
    wk_r = wk.rearrange("(kc kp) o -> kp kc o", kp=128)
    wv_r = wv.rearrange("(kc kp) o -> kp kc o", kp=128)
    wo_r = wo.rearrange("(kc kp) o -> kp kc o", kp=128)
    v_wq = dma(GP, wqs[:].rearrange("p (kc o) -> p kc o", kc=DC), wq_r, dWq)
    v_wk = dma(GP, wks[:].rearrange("p (kc o) -> p kc o", kc=DC), wk_r, dWk)
    v_wv = dma(GP, wvs[:].rearrange("p (kc o) -> p kc o", kc=DC), wv_r, dWv)
    v_wo = dma(GP, wos[:].rearrange("p (kc o) -> p kc o", kc=DC), wo_r, dWo)
    v_pw1 = 16
    v_pw2 = 16
    v_ph1 = 112
    v_ph2 = 128
    v_pw3 = 32

    def prep_w1():
        dma(GP, w1b, w1t, dPw1)

    def prep_w2():
        dma(GP, w2b, w2t, dPw2)

    def prep_h1(j, half=None):
        if half is None:
            dma(GP, h1b[j * 8:(j + 1) * 8], h1t[j * 8:(j + 1) * 8], dPh1)
        else:
            o = j * 8 + half * 4
            dma(GP, h1b[o:o + 4], h1t[o:o + 4], dPh1)

    def prep_h2(j, half=None):
        if half is None:
            dma(GP, h2b[j * 8:(j + 1) * 8], h2t[j * 8:(j + 1) * 8], dPh2)
        else:
            o = j * 8 + half * 4
            dma(GP, h2b[o:o + 4], h2t[o:o + 4], dPh2)

    def prep_w3(j):
        dma(GP, w3b[j * 4:(j + 1) * 4], w3t[j * 4:(j + 1) * 4], dPw3)

    VE.wait(dAux, v_aux)
    VE.do(lambda e: e.memset(onesb[:], 1.0), cDVE)
    VE.do(lambda e: e.memset(ones1f[:], 1.0), cDVE)
    VE.do(lambda e: e.memset(ones1b[:], 1.0), cDVE)
    VE.do(lambda e: e.memset(ones512b[:], 1.0), cDVE)

    def rows(m_psum, sq_psum, g_wait):
        VE.wait(cPE, g_wait)
        VE.do(lambda e, o=r0[:], i=m_psum: e.tensor_scalar_mul(o, i, 1.0 / D), cDVE)
        VE.do(lambda e, o=r1[:], a=r0[:]: e.tensor_tensor(o, a, a, ALU.mult), cDVE)
        VE.do(lambda e, o=r2[:], i=r1[:]: e.tensor_single_scalar(o, i, EPS_LN, ALU.subtract), cDVE)
        e_var = VE.do(lambda e, o=r1[:], i=sq_psum, m=r2[:]:
                      e.scalar_tensor_tensor(o, i, 1.0 / D, m, ALU.mult, ALU.subtract), cDVE)
        SC.wait(cDVE, e_var)
        SC.do(lambda e, o=r2[:], i=r1[:]: e.activation(o, i, AF.Ln), cACT)
        e_exp = SC.do(lambda e, o=r1[:], i=r2[:]: e.activation(o, i, AF.Exp, scale=-0.5), cACT)
        VE.wait(cACT, e_exp)
        e_mr = VE.do(lambda e, o=r2[:], a=r0[:], b=r1[:]: e.tensor_tensor(o, a, b, ALU.mult), cDVE)
        return e_mr, e_exp

    def bcast(e_mr, e_exp, pe_waits):
        PE.waits(pe_waits)
        PE.wait(cDVE, e_mr)
        PE.wait(cACT, e_exp)
        PE.do(lambda e, o=PB0, l=ones1f[:], r=r1[:]: e.matmul(o, l, r, start=True, stop=True), cPE)
        g = PE.do(lambda e, o=PB1, l=ones1f[:], r=r2[:]: e.matmul(o, l, r, start=True, stop=True), cPE)
        VE.wait(cPE, g)
        VE.do(lambda e, o=Rb[:], i=PB0: e.tensor_copy(o, i), cDVE)
        e_c = VE.do(lambda e, o=MRb[:], i=PB1: e.tensor_copy(o, i), cDVE)
        return e_c, g

    e_xbf = [0] * nt
    e_xsq = [0] * nt
    e_ln1 = [0] * nt
    e_qstt = [[0] * DC for _ in range(nt)]
    e_kstt = [[0] * 4 for _ in range(nt)]
    e_vact = [[0] * 4 for _ in range(nt)]
    e_kcol = [0] * nt
    g_kv_ref = [0]
    g_kv_last = 0
    e_kvT = 0
    e_zlT = 0
    prologue_hooks = []
    v_qs_w = [0] * nt
    v_x = [0] * (nt + 1)
    paf = [(cDVE, 0)]
    pbf = [(cDVE, 0)]
    pcf = [(cDVE, 0)]

    v_x[0] = v_x0
    prep_sched = [(0, prep_w1), (2, prep_w2),
                  (3, lambda: prep_h1(0, 0)), (4, lambda: prep_h1(0, 1)),
                  (5, lambda: prep_h1(1, 0)), (6, lambda: prep_h1(1, 1)),
                  (7, lambda: prep_h1(2, 0))]


    def emit_casts(i):
        VE.wait(dX, v_x[i])
        e_xbf[i] = VE.do(lambda e, o=xbf[:], s=xslab[:]: e.tensor_copy(o, s), cDVE)
        SC.wait(dX, v_x[i])
        e_xsq[i] = SC.do(lambda e, o=xsq[:], s=xslab[:]: e.activation(o, s, AF.Square), cACT)

    v_q2 = [0] * (nt + 1)
    v_x2 = [0] * (nt + 1)
    v_out = [0] * nt
    e_z = [0] * (nt + 1)
    e_scoTT = [[0] * DC for _ in range(nt + 1)]
    e_asq_l = [0] * (nt + 1)
    e_yupd_l = [0] * (nt + 1)
    e_ysq = [0] * (nt + 1)
    e_ln2_l = [0] * (nt + 1)
    e_out_l = [0] * nt
    g_ao_l = [0] * (nt + 1)
    g_w1w2_l = [0] * nt
    rotb = [PA0, PA1, PD0, PD1]
    rot_free = [[(cACT, e_vact[nt - 1][2])], [(cACT, e_vact[nt - 1][2])],
                [(cDVE, e_kvT)], [(cDVE, e_kvT)]]
    rot_i = 0
    pc_free = [(cDVE, e_kcol[nt - 1])]
    pb_free = [(cACT, e_vact[nt - 1][3]), (cDVE, e_ln1[nt - 1])]
    slot_ap = [pan0[:, 0:2048], pan0[:, 2048:4096], pan1[:, 0:2048],
               pan1[:, 2048:4096], pan2[:, 0:2048], pan2[:, 2048:4096]]
    NSLOT = 6
    slot_free = [[(cDVE, 0)] for _ in range(NSLOT)]
    slot_v = [0] * NSLOT
    s_idx = 0

    def load_slot(src_ap, a_split, wait_extra):
        nonlocal s_idx
        s = s_idx % NSLOT
        s_idx += 1
        SY.waits(slot_free[s])
        SY.waits(wait_extra)
        dst = slot_ap[s].rearrange("p (a b) -> p a b", a=a_split)
        slot_v[s] = dma(SY, dst, src_ap, dS[s])
        return s

    def rot_next(waits=()):
        nonlocal rot_i
        b = rot_i % 4
        rot_i += 1
        PE.waits(rot_free[b])
        PE.waits(waits)
        return b


    def load_qx(j):
        if j == 0:
            SY.wait(dQS, 16 * (nt - 1))
        SY.wait(cPE, g_ao_l[j - 1] if j > 0 else g_kv_last)
        v_q2[j] = dma(SY, qt[:].rearrange("p (c n) -> p c n", c=DC),
                      qscr[j].rearrange("c p n -> p c n"), dQ)
        SY.wait(cDVE, e_yupd_l[j - 1] if j > 0 else e_ln1[nt - 1])
        v_x2[j] = dma(SY, Xsl[:].rearrange("p (c n) -> p c n", c=DC),
                      xtr[:, :, j * T:(j + 1) * T], dX)

    def attnA_hooks(j):
        hooks = []
        st_ = {}

        def h_zden():
            nonlocal pc_free
            PE.wait(dQ, v_q2[j])
            PE.wait(cDVE, e_zlT)
            PE.waits(pc_free)
            for c in range(DC):
                PE.do(lambda e, o=PC[0:16, 0:512], l=zlT[:, c * 16:(c + 1) * 16],
                      r=qt[:, c * T:(c + 1) * T], st=(c == 0), sp=(c == DC - 1):
                      e.matmul(o, l, r, start=st, stop=sp))
            g_zden = PE.inc_last(cPE)
            VE.wait(cPE, g_zden)
            st_["e_zread"] = VE.do(lambda e, o=su[0:16, 512:1024], p=PC[0:16, 0:512]:
                                   e.tensor_single_scalar(o, p, EPS_Z, ALU.add), cDVE)
            VE.do(lambda e, o=su[0:16, 1536:2048], i_=su[0:16, 512:1024]:
                  e.reciprocal(o, i_), cDVE)
            e_z[j] = VE.do(lambda e, o=zrb[:], i_=su[0:16, 1536:2048]: e.tensor_copy(o, i_), cDVE)
            st_["e_sv"] = [0] * DC
            st_["e_zbc"] = [0] * DC
        hooks.append(h_zden)

        def mk_sraw(c):
            def f():
                scb = PC0 if c % 2 == 0 else PC1
                if c == 0:
                    PE.wait(cDVE, st_["e_zread"])
                if c >= 2:
                    PE.wait(cACT, st_["e_sv"][c - 2])
                g_sr = PE.do(lambda e, o=scb, l=kvT[:, c * 128:(c + 1) * 128],
                             r=qt[:, c * T:(c + 1) * T]:
                             e.matmul(o, l, r, start=True, stop=True), cPE)
                SC.wait(cPE, g_sr)
                st_["e_sv"][c] = SC.do(lambda e, o=qt[:, c * T:(c + 1) * T], p=scb:
                                       e.activation(o, p, AF.Copy), cACT)
            return f
        for c in range(DC):
            hooks.append(mk_sraw(c))

        def mk_zb(c):
            def f():
                nonlocal pb_free
                zh = PB0 if c % 2 == 0 else PB1
                PE.wait(cDVE, e_z[j])
                if c < 2:
                    PE.waits(pb_free)
                else:
                    PE.wait(cDVE, st_["e_zbc"][c - 2])
                g_zb = PE.do(lambda e, o=zh, l=sel[:, c * 128:(c + 1) * 128], r=zrb[:]:
                             e.matmul(o, l, r, start=True, stop=True), cPE)
                VE.wait(cPE, g_zb)
                st_["e_zbc"][c] = VE.do(lambda e, o=zbs[:], p=zh: e.tensor_copy(o, p), cDVE)
                VE.wait(cACT, st_["e_sv"][c])
                e_scoTT[j][c] = VE.do(lambda e, o=qt[:, c * T:(c + 1) * T], b=zbs[:]:
                                      e.tensor_tensor(o, o, b, ALU.mult), cDVE)
            return f
        for c in range(DC):
            hooks.append(mk_zb(c))

        st_["e_asq"] = [0] * DC

        def mk_ao(oc):
            def f():
                half = PC0 if oc % 2 == 0 else PC1
                PE.wait(dWo, v_wo)
                PE.wait(cDVE, e_scoTT[j][DC - 1])
                if oc >= 2:
                    PE.wait(cACT, st_["e_asq"][oc - 2])
                for kc in range(DC):
                    PE.do(lambda e, o=half, l=wos[:, kc * D + oc * 128: kc * D + (oc + 1) * 128],
                          r=qt[:, kc * T:(kc + 1) * T], st=(kc == 0), sp=(kc == DC - 1):
                          e.matmul(o, l, r, start=st, stop=sp))
                g_ao = PE.inc_last(cPE)
                SC.wait(cPE, g_ao)
                if j == 0:
                    SC.wait(dQS, 16 * nt)
                else:
                    SC.wait(cDVE, e_yupd_l[j - 1])
                    SC.wait(cPE, g_w1w2_l[j - 1])
                SC.do(lambda e, o=aosl[:, oc * T:(oc + 1) * T], p=half,
                      b=bos[:, oc:oc + 1]: e.activation(o, p, AF.Identity, bias=b), cACT)
                st_["e_asq"][oc] = SC.do(lambda e, o=shsl[:, oc * T:(oc + 1) * T], p=half,
                                         b=bos[:, oc:oc + 1]:
                                         e.activation(o, p, AF.Square, bias=b), cACT)
                if oc == DC - 1:
                    g_ao_l[j] = cPE.n
                    e_asq_l[j] = st_["e_asq"][DC - 1]
            return f
        for oc in range(DC):
            hooks.append(mk_ao(oc))

        def h_lnattn():
            nonlocal pc_free, pb_free
            pc_free = [(cACT, e_asq_l[j])]
            PE.wait(cACT, e_asq_l[j])
            PE.wait(cDVE, st_["e_zbc"][DC - 1])
            for kc in range(DC):
                PE.do(lambda e, o=PB[0:1, 0:512], l=onesb[:], r=aosl[:, kc * T:(kc + 1) * T],
                      st=(kc == 0), sp=(kc == DC - 1): e.matmul(o, l, r, start=st, stop=sp))
            for kc in range(DC):
                PE.do(lambda e, o=PB[0:1, 512:1024], l=onesb[:], r=shsl[:, kc * T:(kc + 1) * T],
                      st=(kc == 0), sp=(kc == DC - 1): e.matmul(o, l, r, start=st, stop=sp))
            g_sa2 = PE.inc_last(cPE)
            st_["rows"] = rows(PB[0:1, 0:512], PB[0:1, 512:1024], g_sa2)
        hooks.append(h_lnattn)

        def h_ybc():
            nonlocal pb_free
            e_mr, e_exp = st_["rows"]
            e_bc, _ = bcast(e_mr, e_exp, [])
            Yb = Yb0 if j % 2 == 0 else Yb1
            VE.wait(cDVE, e_bc)
            VE.wait(dX, v_x2[j])
            if j >= 2:
                VE.wait(cDVE, e_out_l[j - 2])
            for c in range(DC):
                VE.do(lambda e, o=zbs[:], a=aosl[:, c * T:(c + 1) * T], b=Rb[:]:
                      e.tensor_tensor(o, a, b, ALU.mult), cDVE)
                VE.do(lambda e, o=su[:, 1024:1536], a=zbs[:], b=MRb[:]:
                      e.tensor_tensor(o, a, b, ALU.subtract), cDVE)
                VE.do(lambda e, o=Yb[:, c * T:(c + 1) * T],
                      a=Xsl[:, c * T:(c + 1) * T], b=su[:, 1024:1536]:
                      e.tensor_tensor(o, a, b, ALU.add), cDVE)
                VE.do(lambda e, o=shsl[:, c * T:(c + 1) * T], a=Yb[:, c * T:(c + 1) * T]:
                      e.tensor_tensor(o, a, a, ALU.mult), cDVE)
            e_yupd_l[j] = cDVE.n
            e_ysq[j] = cDVE.n
            pb_free = [(cDVE, e_yupd_l[j])]
        hooks.append(h_ybc)
        return hooks


    def emit_boundary():
        nonlocal g_kv_last, e_kvT, e_zlT
        g_kv_last = g_kv_ref[0]
        VE.wait(cPE, g_kv_last)
        VE.do(lambda e: e.memset(kvT[:], 0.0), cDVE)
        for c in range(DC):
            VE.do(lambda e, o=kvT[0:64, c * 128: c * 128 + 64],
                  p=PD[0:64, c * 128: c * 128 + 64]: e.tensor_copy(o, p), cDVE)
            VE.do(lambda e, o=kvT[64:128, c * 128 + 64: c * 128 + 128],
                  p=PD[64:128, c * 128 + 64: c * 128 + 128]: e.tensor_copy(o, p), cDVE)
        e_kvT = cDVE.n
        kcol_fin = kcol0 if (nt - 1) % 2 == 0 else kcol1
        for c in range(DC):
            VE.do(lambda e, o=zlT[:, c * 16:(c + 1) * 16], i_=ind[:, c * 16:(c + 1) * 16],
                  s=kcol_fin[:, c:c + 1]: e.tensor_scalar_mul(o, i_, s), cDVE)
        e_zlT = cDVE.n



    emit_casts(0)
    pend_kv = None
    for i in range(nt):
        PE.scope(f"p1t{i}_ln")
        PE.wait(cDVE, e_xbf[i])
        PE.wait(cACT, e_xsq[i])
        PE.waits(pbf)
        for kc in range(DC):
            PE.do(lambda e, o=PB[0:1, 0:512], l=onesb[:], r=xbf[:, kc * T:(kc + 1) * T],
                  st=(kc == 0), sp=(kc == DC - 1): e.matmul(o, l, r, start=st, stop=sp))
        g_sx = PE.inc_last(cPE)
        for kc in range(DC):
            PE.do(lambda e, o=PB[0:1, 512:1024], l=onesb[:], r=xsq[:, kc * T:(kc + 1) * T],
                  st=(kc == 0), sp=(kc == DC - 1): e.matmul(o, l, r, start=st, stop=sp))
        g_sx2 = PE.inc_last(cPE)
        if pend_kv is not None:
            pend_kv[0]()
        e_mr, e_exp = rows(PB[0:1, 0:512], PB[0:1, 512:1024], g_sx2)
        e_bc, g_bc = bcast(e_mr, e_exp, paf + pbf)
        pbf = [(cDVE, e_mr)]
        VE.wait(cDVE, e_bc)
        for kc in range(DC):
            VE.do(lambda e, o=zbs[:], a=xbf[:, kc * T:(kc + 1) * T], b=Rb[:]:
                  e.tensor_tensor(o, a, b, ALU.mult), cDVE)
            VE.do(lambda e, o=ln1[:, kc * T:(kc + 1) * T], a=zbs[:], b=MRb[:]:
                  e.tensor_tensor(o, a, b, ALU.subtract), cDVE)
        e_ln1[i] = cDVE.n
        paf = [(cDVE, e_ln1[i])]
        pbf = pbf + [(cDVE, e_ln1[i])]
        if i + 1 < nt:
            SY.wait(cDVE, e_xbf[i])
            SY.wait(cACT, e_xsq[i])
            v_x[i + 1] = dma(SY, xsl3, xtr[:, :, (i + 1) * T:(i + 2) * T], dX)
            emit_casts(i + 1)
        if pend_kv is not None:
            pend_kv[1]()
            pend_kv = None
        def emit_q(hooks=None, qb0=None, qb1=None):
            PE.scope(f"p1t{i}_q")
            PE.wait(cDVE, e_ln1[i])
            PE.wait(dWq, v_wq)
            PE.wait(dWrest, v_wrest)
            if i > 0:
                PE.wait(cDVE, e_kcol[i - 1])
            for oc in range(DC):
                half = (qb0 or PC0) if oc % 2 == 0 else (qb1 or PC1)
                if oc >= 2:
                    PE.wait(cDVE, e_qstt[i][oc - 2])
                if i == nt - 1:
                    PE.wait(cDVE, e_kcol[i])
                PE.do(lambda e, o=half, l=bqb[0:1, oc * 128:(oc + 1) * 128], r=ones512b[:]:
                      e.matmul(o, l, r, start=True, stop=False))
                for kc in range(DC):
                    PE.do(lambda e, o=half, l=wqs[:, kc * D + oc * 128: kc * D + (oc + 1) * 128],
                          r=ln1[:, kc * T:(kc + 1) * T], sp=(kc == DC - 1):
                          e.matmul(o, l, r, start=False, stop=sp))
                g_q = PE.inc_last(cPE)
                e_s = su[:, 0:512] if oc % 2 == 0 else su[:, 1024:1536]
                SC.wait(cPE, g_q)
                if i == nt - 1 and oc < 2:
                    SC.wait(cDVE, e_kstt[i][3])
                if oc >= 2:
                    SC.wait(cDVE, e_qstt[i][oc - 2])
                e_e = SC.do(lambda e, o=e_s, p=half: e.activation(o, p, AF.Exp), cACT)
                VE.wait(cACT, e_e)
                VE.do(lambda e, o=e_s: e.tensor_scalar_min(o, o, 1.0), cDVE)
                if i > 0:
                    VE.wait(dQS, v_qs_w[i - 1])
                e_qstt[i][oc] = VE.do(
                    lambda e, o=qslab[:, oc * T:(oc + 1) * T], p=half, m=e_s:
                    e.scalar_tensor_tensor(o, p, 0.0, m, ALU.max, ALU.add), cDVE)
                if hooks:
                    hooks.pop(0)()
                if hooks:
                    hooks.pop(0)()

        def emit_k():
            nonlocal paf, pbf
            PE.scope(f"p1t{i}_k")
            PE.wait(dWk, v_wk)
            for ns in range(4):
                bank = PA if ns % 2 == 0 else PB
                PE.waits(paf if ns % 2 == 0 else pbf)
                PE.do(lambda e, o=bank[:, 0:512], l=ones1b[:], r=bkb[0:1, 0:512]:
                      e.matmul(o, l, r, start=True, stop=False))
                PE.do(lambda e, o=bank[:, 512:1024], l=ones1b[:], r=bkb[0:1, 512:1024]:
                      e.matmul(o, l, r, start=True, stop=False))
                for kc in range(DC):
                    for h in range(2):
                        PE.do(lambda e, o=bank[:, h * 512:(h + 1) * 512],
                              l=ln1[:, kc * T + ns * 128: kc * T + (ns + 1) * 128],
                              r=wks[:, kc * D + h * 512: kc * D + (h + 1) * 512],
                              sp=(kc == DC - 1): e.matmul(o, l, r, start=False, stop=sp))
                g_k = PE.inc_last(cPE)
                eh = su[:, 0:1024] if ns % 2 == 0 else su[:, 1024:2048]
                SC.wait(cPE, g_k)
                SC.wait(cDVE, e_qstt[i][DC - 1])
                if ns >= 2:
                    SC.wait(cDVE, e_kstt[i][ns - 2])
                e_e = SC.do(lambda e, o=eh, p=bank[:]: e.activation(o, p, AF.Exp), cACT)
                VE.wait(cACT, e_e)
                VE.do(lambda e, o=eh: e.tensor_scalar_min(o, o, 1.0), cDVE)
                e_kstt[i][ns] = VE.do(
                    lambda e, o=kslab[:, ns * D:(ns + 1) * D], p=bank[:], m=eh:
                    e.scalar_tensor_tensor(o, p, 0.0, m, ALU.max, ALU.add), cDVE)
                if ns % 2 == 0:
                    paf = [(cACT, e_e), (cDVE, e_kstt[i][ns])]
                else:
                    pbf = [(cACT, e_e), (cDVE, e_kstt[i][ns])]

        def emit_v():
            nonlocal paf, pbf
            PE.scope(f"p1t{i}_v")
            PE.wait(dWv, v_wv)
            for ns in range(4):
                bank = PA if ns % 2 == 0 else PB
                PE.waits(paf if ns % 2 == 0 else pbf)
                PE.do(lambda e, o=bank[:, 0:512], l=ones1b[:], r=bvb[0:1, 0:512]:
                      e.matmul(o, l, r, start=True, stop=False))
                PE.do(lambda e, o=bank[:, 512:1024], l=ones1b[:], r=bvb[0:1, 512:1024]:
                      e.matmul(o, l, r, start=True, stop=False))
                for kc in range(DC):
                    for h in range(2):
                        PE.do(lambda e, o=bank[:, h * 512:(h + 1) * 512],
                              l=ln1[:, kc * T + ns * 128: kc * T + (ns + 1) * 128],
                              r=wvs[:, kc * D + h * 512: kc * D + (h + 1) * 512],
                              sp=(kc == DC - 1): e.matmul(o, l, r, start=False, stop=sp))
                g_v = PE.inc_last(cPE)
                SC.wait(cPE, g_v)
                e_vact[i][ns] = SC.do(lambda e, o=vslab[:, ns * D:(ns + 1) * D], p=bank[:]:
                                      e.activation(o, p, AF.Copy), cACT)
                if ns % 2 == 0:
                    paf = [(cACT, e_vact[i][ns])]
                else:
                    pbf = [(cACT, e_vact[i][ns])]

        if i < nt - 1:
            emit_q()
            emit_k()
            emit_v()
        else:
            emit_k()
            emit_v()
        def mk_kv(i):
            def kv_part():
                PE.scope(f"p1t{i}_kv")
                for ns in range(4):
                    PE.wait(cDVE, e_kstt[i][ns])
                    PE.wait(cACT, e_vact[i][ns])
                    for c in range(DC):
                        PE.do(lambda e, o=PD[:, c * 128:(c + 1) * 128],
                              l=kslab[:, ns * D + c * 128: ns * D + (c + 1) * 128],
                              r=vslab[:, ns * D + c * 128: ns * D + (c + 1) * 128],
                              st=(i == 0 and ns == 0 and c % 4 == 0),
                              sp=(i == nt - 1 and ns == 3 and c % 4 == 3):
                              e.matmul(o, l, r, start=st, stop=sp))
                nonlocal_g = PE.inc_last(cPE)
                g_kv_ref[0] = nonlocal_g

            def ks_part():
                PE.scope(f"p1t{i}_ks")
                if i == nt - 1 and i > 0:
                    PE.wait(cDVE, max(e_qstt[i - 1][DC - 1], e_kcol[i - 1]))
                else:
                    PE.wait(cDVE, e_qstt[i][DC - 1])
                for ns in range(4):
                    for c in range(DC):
                        PE.do(lambda e, o=PC[:, 512 + c:513 + c],
                              l=kslab[:, ns * D + c * 128: ns * D + (c + 1) * 128],
                              r=onesb[:], st=(ns == 0 and c == 0), sp=(ns == 3 and c == DC - 1):
                              e.matmul(o, l, r, start=st, stop=sp))
                g_ks = PE.inc_last(cPE)
                VE.wait(cPE, g_ks)
                if i == 0:
                    e_kcol[i] = VE.do(lambda e, o=kcol0[:], p=PC[:, 512:520]: e.tensor_copy(o, p), cDVE)
                else:
                    sr = kcol0 if (i - 1) % 2 == 0 else kcol1
                    ds = kcol0 if i % 2 == 0 else kcol1
                    e_kcol[i] = VE.do(lambda e, o=ds[:], a=sr[:], b=PC[:, 512:520]:
                                      e.tensor_tensor(o, a, b, ALU.add), cDVE)
            return (kv_part, ks_part)

        pend_kv = mk_kv(i)
        if i == nt - 1:
            pend_kv[0]()
            pend_kv[1]()
            pend_kv = None
            emit_boundary()
            load_qx(0)
            pc_free = [(cDVE, e_kcol[i])]
            pb_free = [(cACT, e_vact[i][3]), (cDVE, e_ln1[i])]
            hooks0 = attnA_hooks(0)
            emit_q(hooks0, qb0=PA0, qb1=PA1)
            prologue_hooks.extend(hooks0)
        pcf_prev_i = i
        SY.wait(cDVE, e_qstt[i][DC - 1])
        v_qs_w[i] = dma(SY, qscr[i].rearrange("c p n -> p c n"),
                        qslab[:].rearrange("p (c n) -> p c n", c=DC), dQS)
        for when, fn in prep_sched:
            if fn is not None and min(when, nt - 1) == i:
                if i == 0:
                    GP.wait(cDVE, e_xbf[0])
                elif i > 0:
                    GP.wait(cDVE, e_kcol[i - 1])
                fn()
        prep_sched = [(w, None if min(w, nt - 1) <= i else f) for w, f in prep_sched]

    def attnB_hooks(j):
        st_ = {}

        def h_stats():
            nonlocal pb_free
            Yb = Yb0 if j % 2 == 0 else Yb1
            PE.wait(cDVE, e_ysq[j])
            PE.waits(pb_free)
            for kc in range(DC):
                PE.do(lambda e, o=PB[0:1, 0:512], l=onesb[:], r=Yb[:, kc * T:(kc + 1) * T],
                      st=(kc == 0), sp=(kc == DC - 1): e.matmul(o, l, r, start=st, stop=sp))
            for kc in range(DC):
                PE.do(lambda e, o=PB[0:1, 512:1024], l=onesb[:], r=shsl[:, kc * T:(kc + 1) * T],
                      st=(kc == 0), sp=(kc == DC - 1): e.matmul(o, l, r, start=st, stop=sp))
            st_["g"] = PE.inc_last(cPE)
            st_["rows"] = rows(PB[0:1, 0:512], PB[0:1, 512:1024], st_["g"])

        def h_apply():
            nonlocal pb_free
            Yb = Yb0 if j % 2 == 0 else Yb1
            e_mr2, e_exp2 = st_["rows"]
            e_bc2, _ = bcast(e_mr2, e_exp2, [])
            VE.wait(cDVE, e_bc2)
            VE.wait(cPE, st_["g"])
            for c in range(DC):
                VE.do(lambda e, o=zbs[:], a=Yb[:, c * T:(c + 1) * T], b=Rb[:]:
                      e.tensor_tensor(o, a, b, ALU.mult), cDVE)
                VE.do(lambda e, o=shsl[:, c * T:(c + 1) * T], a=zbs[:], b=MRb[:]:
                      e.tensor_tensor(o, a, b, ALU.subtract), cDVE)
            e_ln2_l[j] = cDVE.n
            pb_free = [(cDVE, e_ln2_l[j])]
        return [h_stats, h_apply]

    rot_free[0] = [(cDVE, e_qstt[nt - 1][DC - 2])]
    rot_free[1] = [(cDVE, e_qstt[nt - 1][DC - 1])]
    rot_free[2] = [(cDVE, e_kvT)]
    rot_free[3] = [(cDVE, e_kvT)]
    PE.scope("prologue")
    while prologue_hooks:
        prologue_hooks.pop(0)()
    for h_ in attnB_hooks(0):
        h_()
    GP.wait(cDVE, e_yupd_l[0])
    prep_h1(2, 1)
    prep_h1(3)

    for i in range(nt):
        Ybi = Yb0 if i % 2 == 0 else Yb1
        if i + 1 < nt:
            load_qx(i + 1)
        PE.scope(f"r{i}_w1w2")
        e_h = [0] * HC
        g_u = g_w = 0
        for g8 in range(16):
            s1_ = load_slot(w1b[g8 * 2:(g8 + 1) * 2].rearrange("a p b c -> p a (b c)"),
                            2, [(dPw1, v_pw1)])
            s2_ = load_slot(w2b[g8 * 2:(g8 + 1) * 2].rearrange("a p b c -> p a (b c)"),
                            2, [(dPw2, v_pw2)])
            for jj in range(2):
                oc = g8 * 2 + jj
                ru = rot_next([(dS[s1_], slot_v[s1_]), (cDVE, e_ln2_l[i])])
                for kc in range(DC):
                    PE.do(lambda e, o=rotb[ru], p=slot_ap[s1_],
                          lo=jj * 1024 + kc * 128, r=shsl[:, kc * T:(kc + 1) * T],
                          st=(kc == 0), sp=(kc == DC - 1):
                          e.matmul(o, p[:, lo:lo + 128], r, start=st, stop=sp))
                g_u = PE.inc_last(cPE)
                rw = rot_next([(dS[s2_], slot_v[s2_])])
                for kc in range(DC):
                    PE.do(lambda e, o=rotb[rw], p=slot_ap[s2_],
                          lo=jj * 1024 + kc * 128, r=shsl[:, kc * T:(kc + 1) * T],
                          st=(kc == 0), sp=(kc == DC - 1):
                          e.matmul(o, p[:, lo:lo + 128], r, start=st, stop=sp))
                g_w = PE.inc_last(cPE)
                sh_ = su[:, 0:512] if oc % 2 == 0 else su[:, 512:1024]
                SC.wait(cPE, g_u)
                if oc >= 2:
                    SC.wait(cDVE, e_h[oc - 2])
                e_si = SC.do(lambda e, o=sh_, p=rotb[ru], s=s1s[:, oc:oc + 1]:
                             e.activation(o, p, AF.Silu, scale=s), cACT)
                VE.wait(cACT, e_si)
                VE.wait(cPE, g_w)
                e_h[oc] = VE.do(lambda e, o=hsl[:, oc * T:(oc + 1) * T], p=rotb[rw],
                                s=s2s[:, oc:oc + 1], m=sh_:
                                e.scalar_tensor_tensor(o, p, s, m, ALU.mult, ALU.mult), cDVE)
                rot_free[ru] = [(cACT, e_si)]
                rot_free[rw] = [(cDVE, e_h[oc])]
            slot_free[s1_] = [(cPE, g_u)]
            slot_free[s2_] = [(cPE, g_w)]
        PE.scope(f"r{i}_h1")
        g_w1w2_l[i] = g_w
        e_g = [0] * HC
        for oc in range(HC):
            sa = load_slot(h1b[oc][:, 0:16, :], 16, [(dPh1, v_ph1)])
            sb = load_slot(h1b[oc][:, 16:32, :], 16, [(dPh1, v_ph1)])
            r_ = rot_next([(dS[sa], slot_v[sa]), (dS[sb], slot_v[sb]),
                           (cDVE, e_h[HC - 1])])
            for kc in range(HC):
                sl = slot_ap[sa] if kc < 16 else slot_ap[sb]
                PE.do(lambda e, o=rotb[r_], p=sl, lo=(kc % 16) * 128,
                      r=hsl[:, kc * T:(kc + 1) * T], st=(kc == 0), sp=(kc == HC - 1):
                      e.matmul(o, p[:, lo:lo + 128], r, start=st, stop=sp))
            g_ = PE.inc_last(cPE)
            SC.wait(cPE, g_)
            if i > 0 and oc < 16:
                SC.wait(cDVE, e_out_l[i - 1])
            if i > 0 and oc == 16:
                SC.wait(dOut, v_out[i - 1])
            e_g[oc] = SC.do(lambda e, o=gsl[:, oc * T:(oc + 1) * T], p=rotb[r_],
                            s=sh1s[:, oc:oc + 1]: e.activation(o, p, AF.Gelu, scale=s), cACT)
            rot_free[r_] = [(cACT, e_g[oc])]
            slot_free[sa] = [(cPE, g_)]
            slot_free[sb] = [(cPE, g_)]
            if i == 0 and oc in (2, 6, 10, 14, 18, 22, 26, 30):
                GP.wait(cPE, g_)
                prep_h2(oc // 8, (oc % 8) // 4)
        hooksA = attnA_hooks(i + 1) if i + 1 < nt else []
        PE.scope(f"r{i}_h2")
        e_h2 = [0] * HC
        for oc in range(HC):
            if oc >= 2 and hooksA:
                hooksA.pop(0)()
            if oc >= 2 and hooksA:
                hooksA.pop(0)()
            sa = load_slot(h2b[oc][:, 0:16, :], 16, [(dPh2, v_ph2)])
            sb = load_slot(h2b[oc][:, 16:32, :], 16, [(dPh2, v_ph2)])
            r_ = rot_next([(dS[sa], slot_v[sa]), (dS[sb], slot_v[sb]),
                           (cACT, e_g[HC - 1])])
            for kc in range(HC):
                sl = slot_ap[sa] if kc < 16 else slot_ap[sb]
                PE.do(lambda e, o=rotb[r_], p=sl, lo=(kc % 16) * 128,
                      r=gsl[:, kc * T:(kc + 1) * T], st=(kc == 0), sp=(kc == HC - 1):
                      e.matmul(o, p[:, lo:lo + 128], r, start=st, stop=sp))
            g_ = PE.inc_last(cPE)
            SC.wait(cPE, g_)
            e_h2[oc] = SC.do(lambda e, o=hsl[:, oc * T:(oc + 1) * T], p=rotb[r_],
                             s=sh2s[:, oc:oc + 1]: e.activation(o, p, AF.Gelu, scale=s), cACT)
            rot_free[r_] = [(cACT, e_h2[oc])]
            slot_free[sa] = [(cPE, g_)]
            slot_free[sb] = [(cPE, g_)]
            if i == 0 and oc in (2, 10):
                GP.wait(cPE, g_)
                prep_w3(oc // 8)
        while hooksA:
            hooksA.pop(0)()
        hooksB = attnB_hooks(i + 1) if i + 1 < nt else []
        PE.scope(f"r{i}_w3")
        e_fsq = [0] * DC
        g_w3 = [0] * DC
        for oc in range(DC):
            if oc in (0, 3) and hooksB:
                hooksB.pop(0)()
            sa = load_slot(w3b[oc][:, 0:16, :], 16, [(dPw3, v_pw3)])
            sb = load_slot(w3b[oc][:, 16:32, :], 16, [(dPw3, v_pw3)])
            r_ = rot_next([(dS[sa], slot_v[sa]), (dS[sb], slot_v[sb]),
                           (cACT, e_h2[HC - 1])])
            for kc in range(HC):
                sl = slot_ap[sa] if kc < 16 else slot_ap[sb]
                PE.do(lambda e, o=rotb[r_], p=sl, lo=(kc % 16) * 128,
                      r=hsl[:, kc * T:(kc + 1) * T], st=(kc == 0), sp=(kc == HC - 1):
                      e.matmul(o, p[:, lo:lo + 128], r, start=st, stop=sp))
            g_w3[oc] = PE.inc_last(cPE)
            SC.wait(cPE, g_w3[oc])
            SC.do(lambda e, o=gsl[:, oc * T:(oc + 1) * T], p=rotb[r_],
                  s=s3s[:, oc:oc + 1]: e.activation(o, p, AF.Identity, scale=s), cACT)
            e_fsq[oc] = SC.do(lambda e, o=gsl[:, HID + oc * T: HID + (oc + 1) * T],
                              p=rotb[r_], s=s3s[:, oc:oc + 1]:
                              e.activation(o, p, AF.Square, scale=s), cACT)
            rot_free[r_] = [(cACT, e_fsq[oc])]
            slot_free[sa] = [(cPE, g_w3[oc])]
            slot_free[sb] = [(cPE, g_w3[oc])]
        while hooksB:
            hooksB.pop(0)()
        while hooksB:
            hooksB.pop(0)()
        PE.scope(f"r{i}_lnf")
        PE.wait(cACT, e_fsq[DC - 1])
        PE.waits(pb_free)
        for kc in range(DC):
            PE.do(lambda e, o=PB[0:1, 0:512], l=onesb[:], r=gsl[:, kc * T:(kc + 1) * T],
                  st=(kc == 0), sp=(kc == DC - 1): e.matmul(o, l, r, start=st, stop=sp))
        for kc in range(DC):
            PE.do(lambda e, o=PB[0:1, 512:1024], l=onesb[:],
                  r=gsl[:, HID + kc * T: HID + (kc + 1) * T],
                  st=(kc == 0), sp=(kc == DC - 1): e.matmul(o, l, r, start=st, stop=sp))
        g_sf2 = PE.inc_last(cPE)
        e_mr3, e_exp3 = rows(PB[0:1, 0:512], PB[0:1, 512:1024], g_sf2)
        e_bc3, _ = bcast(e_mr3, e_exp3, [])
        VE.wait(cDVE, e_bc3)
        VE.wait(cPE, g_w3[DC - 1])
        for c in range(DC):
            VE.do(lambda e, o=zbs[:], a=gsl[:, c * T:(c + 1) * T], b=Rb[:]:
                  e.tensor_tensor(o, a, b, ALU.mult), cDVE)
            VE.do(lambda e, o=zbs2[:], a=zbs[:], b=MRb[:]:
                  e.tensor_tensor(o, a, b, ALU.subtract), cDVE)
            VE.do(lambda e, o=outsl[:, c * T:(c + 1) * T],
                  a=Ybi[:, c * T:(c + 1) * T], b=zbs2[:]:
                  e.tensor_tensor(o, a, b, ALU.add), cDVE)
        e_out_l[i] = cDVE.n
        pb_free = [(cDVE, e_out_l[i])]
        SC.wait(cDVE, e_out_l[i])
        v_out[i] = dma(SC, ytr[:, :, i * T:(i + 1) * T],
                       outsl[:].rearrange("p (c n) -> p c n", c=DC), dOut)

    SY.wait(dOut, v_out[nt - 1])

    _simulate([SY, GP, PE, SC, VE], tag=f"nt={nt}")

    def replay(rec, self_sync=False):
        def f(eng):
            last_inc = None
            open_scope = [None]
            for op in rec.ops:
                if op[0] == "s":
                    if open_scope[0] is not None:
                        nc.leave_named_scope(open_scope[0][0], open_scope[0][1], False)
                        open_scope[0] = None
                    if op[1] is not None:
                        sid, _ = nc.enter_named_scope(op[1], False)
                        open_scope[0] = (op[1], sid)
                elif op[0] == "w":
                    eng.wait_ge(op[1].h, op[2])
                else:
                    if self_sync and last_inc is not None:
                        eng.wait_ge(last_inc[0].h, last_inc[1])
                    ins = op[1](eng)
                    if op[2] is not None:
                        ins.then_inc(op[2].h, op[3])
                        op[2].n_replay += op[3]
                        if op[3] == 1:
                            last_inc = (op[2], op[2].n_replay)
            if open_scope[0] is not None:
                nc.leave_named_scope(open_scope[0][0], open_scope[0][1], False)
        return f

    for s_ in all_sems:
        s_.n_replay = 0

    with nc.Block() as block:
        block.sync(replay(SY))
        block.gpsimd(replay(GP))
        block.tensor(replay(PE))
        block.scalar(replay(SC, self_sync=paranoid))
        block.vector(replay(VE, self_sync=paranoid))
    _sem_ctx.close()
    return nc


def _host_layout(inputs, nt=NT):
    x = np.asarray(inputs["L0_attn_x"], dtype=np.float32)
    B = x.shape[0]
    g = lambda k: np.asarray(inputs[k], dtype=np.float32)
    C = np.ascontiguousarray
    tile_kxo = lambda wT, kc, oc: C(wT.reshape(kc, 128, oc, 128).transpose(2, 1, 0, 3))
    common = {
        "wq": C(g("L0_attn_wq")), "wk": C(g("L0_attn_wk")),
        "wv": C(g("L0_attn_wv")), "wo": C(g("L0_attn_wo")),
        "bqr": C(g("L0_attn_bq").reshape(1, D)),
        "bot": C(g("L0_attn_bo").reshape(DC, 128).T),
        "bk": C(g("L0_attn_bk").reshape(1, D)),
        "bv": C(g("L0_attn_bv").reshape(1, D)),
        "w1t": tile_kxo(g("L0_ffn_w1_w").T, DC, HC),
        "w2t": tile_kxo(g("L0_ffn_w2_w").T, DC, HC),
        "h1t": tile_kxo(g("L0_ffn_h1_w").T, HC, HC),
        "h2t": tile_kxo(g("L0_ffn_h2_w").T, HC, HC),
        "w3t": tile_kxo(g("L0_ffn_w3_w").T, HC, DC),
        "s1": C(g("L0_ffn_w1_s").reshape(HC, 128).T),
        "s2": C(g("L0_ffn_w2_s").reshape(HC, 128).T),
        "sh1": C(g("L0_ffn_h1_s").reshape(HC, 128).T),
        "sh2": C(g("L0_ffn_h2_s").reshape(HC, 128).T),
        "s3": C(g("L0_ffn_w3_s").reshape(DC, 128).T),
    }
    selm = np.zeros((16, DC * 128), dtype=np.float32)
    indm = np.zeros((128, DC * 16), dtype=np.float32)
    for c in range(DC):
        selm[2 * c, c * 128: c * 128 + 64] = 1.0
        selm[2 * c + 1, c * 128 + 64: c * 128 + 128] = 1.0
        indm[0:64, c * 16 + 2 * c] = 1.0
        indm[64:128, c * 16 + 2 * c + 1] = 1.0
    common["selc"] = selm
    common["indc"] = indm

    in_maps = []
    for ci in range(B):
        m = dict(common)
        m["xt"] = C(x[ci, : nt * T].T)
        in_maps.append(m)
    return in_maps


def kernel(**inputs):
    nc = build(NT)
    in_maps = _host_layout(inputs, NT)
    res = run_bass_kernel_spmd(nc, in_maps, core_ids=list(range(N_CORES)))
    out = np.stack([np.ascontiguousarray(np.asarray(r["yt"], dtype=np.float32).T)
                    for r in res.results], axis=0)
    return out
